# version 16
# baseline (speedup 1.0000x reference)
import os

import numpy as np
import ml_dtypes

import concourse.bass as bass
import concourse.mybir as mybir
from concourse.bass_isa import ReduceOp
from concourse.bass_utils import run_bass_kernel_spmd

F32 = mybir.dt.float32
BF16 = mybir.dt.bfloat16
I32 = mybir.dt.int32
AX = mybir.AxisListType.X
ALU = mybir.AluOpType
ACT = mybir.ActivationFunctionType

MAGIC = 0x5F3759DF
RND = 12582912.0
QCLIP_LO = -128.4999
QCLIP_HI = 127.4999

H, DK, DV, D, DI, CHK = 16, 64, 128, 1024, 2816, 128
NIB = DI // 128


def host_tables(nch):
    t = {}
    t["ident"] = np.eye(128, dtype=ml_dtypes.bfloat16)
    gamma = 1.0 - np.exp2(-5.0 - np.arange(H, dtype=np.float64))

    inv = 1.0 / 10000.0 ** (np.arange(0, DK, 2, dtype=np.float64) / DK)
    pos = (np.arange(nch)[:, None, None] * 128
           + np.arange(128)[None, :, None]).astype(np.float64)
    f = pos * inv[None, None, :]
    cos = np.cos(f).transpose(1, 0, 2).reshape(128, nch * 32)
    sin = np.sin(f).transpose(1, 0, 2).reshape(128, nch * 32)
    c3 = cos.reshape(128, nch, 32)
    s3 = sin.reshape(128, nch, 32)
    t["ct2"] = np.concatenate([c3, c3], -1).reshape(
        128, nch * 64).astype(ml_dtypes.bfloat16)
    t["st2"] = np.concatenate([-s3, s3], -1).reshape(
        128, nch * 64).astype(ml_dtypes.bfloat16)

    i = np.arange(CHK, dtype=np.float64)
    dq64 = np.zeros((64, H * 128), np.float64)
    for h in range(H):
        dq64[:, h * 128:(h + 1) * 128] = gamma[h] ** (i + 1.0)
    t["dqfm"] = (dq64 / 8.0).astype(ml_dtypes.bfloat16)

    j = np.arange(CHK, dtype=np.float64)[:, None]
    dkv = np.repeat(gamma[None, :], 128, 0) ** (CHK - 1.0 - j)
    t["dkv"] = dkv.astype(ml_dtypes.bfloat16)

    rel = i[None, :] - j
    dt = np.zeros((128, H * 128), np.float64)
    for h in range(H):
        dt[:, h * 128:(h + 1) * 128] = np.where(
            rel >= 0, gamma[h] ** np.maximum(rel, 0.0), 0.0)
    t["dt"] = (dt / 8.0).astype(ml_dtypes.bfloat16)

    t["gC"] = (gamma ** CHK).astype(np.float64)
    t["ones_col"] = np.ones((1, 128), np.float32)
    return t


ENGINES = ("sync", "gpsimd", "vector", "scalar", "tensor")


class Plan:

    def __init__(self, nc):
        self.nc = nc
        self.streams = {e: [] for e in ENGINES}
        self.sems = {}
        self.counts = {}
        self.lastwait = {e: {} for e in ENGINES}
        self.guards = []

    def sem(self, name):
        if name not in self.sems:
            g = self.nc.semaphore(name)
            self.sems[name] = g.__enter__()
            self.guards.append(g)
            self.counts[name] = 0
        return self.sems[name]

    def op(self, eng, build, sig=None, sign=1):
        tok = None
        s = None
        if sig:
            self.sem(sig)
            self.counts[sig] += sign
            tok = (sig, self.counts[sig])
            s = self.sems[sig]

        def fn(e, build=build, s=s, sign=sign):
            ins = build(e)
            if s is not None:
                ins.then_inc(s, sign)
        self.streams[eng].append(fn)
        return tok

    def wait(self, eng, tok):
        if tok is None:
            return
        sig, val = tok
        lw = self.lastwait[eng]
        if lw.get(sig, -1) >= val:
            return
        lw[sig] = val
        s = self.sems[sig]
        self.streams[eng].append(lambda e, s=s, val=val: e.wait_ge(s, val))

    def waits(self, eng, toks):
        for t in toks:
            self.wait(eng, t)

    def dma(self, queue, out, in_, sig, waits=()):
        self.waits(queue, waits)
        self.sem(sig)
        if self.counts[sig] > 0:
            self.wait(queue, (sig, self.counts[sig]))
        return self.op(queue, lambda e: e.dma_start(out=out, in_=in_),
                       sig=sig, sign=16)

    def finalize(self, block):
        for e in ENGINES:
            stream = self.streams[e]
            getattr(block, e)(lambda eng, stream=stream: [f(eng) for f in stream])


def emit_rsqrt(p, i32a, i32b, v_ap, out_ap, y_ap, t_ap):
    def fv(build):
        tok = p.op("vector", build, sig="fv")
        p.wait("vector", tok)
    fv(lambda v: v.tensor_scalar(i32a, v_ap.bitcast(I32), 1, None,
                                 ALU.arith_shift_right))
    fv(lambda v: v.tensor_scalar(i32b, i32a, -1.0, float(MAGIC),
                                 ALU.mult, ALU.add))
    y = i32b.bitcast(F32)
    for it in range(3):
        fv(lambda v, y=y: v.tensor_tensor(t_ap, y, y, ALU.mult))
        fv(lambda v: v.tensor_tensor(t_ap, t_ap, v_ap, ALU.mult))
        fv(lambda v: v.tensor_scalar(t_ap, t_ap, -0.5, 1.5, ALU.mult, ALU.add))
        dst = out_ap if it == 2 else y_ap
        fv(lambda v, y=y, dst=dst: v.tensor_tensor(dst, y, t_ap, ALU.mult))
        y = y_ap


def _fake_resid(p, q2, tx, fr_x, fr_xa, xa_v, x_tok, xa_dram, row0):
    p.wait("vector", tx)
    if fr_xa[q2]:
        p.wait("vector", fr_xa[q2])
    tok = p.op("vector", lambda v, q2=q2: v.tensor_copy(xa_v(q2), x_tok(q2)),
               sig="vg")
    fr_x[q2] = tok
    p.wait("gpsimd", tok)
    p.dma("gpsimd", xa_dram[row0:row0 + 128, :], xa_v(q2), "st_xa%d" % q2)
    fr_xa[q2] = ("st_xa%d" % q2, p.counts["st_xa%d" % q2])


def build_program(nb, nch, dbg=False):
    T = nch * 128
    NTOK = nb * T
    NT512 = NTOK // 512
    assert T % 512 == 0

    tabs = host_tables(nch)
    nc = bass.Bass("TRN2", target_bir_lowering=False, debug=False)
    kind_s = "ExternalOutput" if dbg else "Internal"

    x_hbm = nc.dram_tensor("x", [NTOK, D], F32, kind="ExternalInput").ap()
    wq_hbm = nc.dram_tensor("wq", [D, D], F32, kind="ExternalInput").ap()
    wk_hbm = nc.dram_tensor("wk", [D, D], F32, kind="ExternalInput").ap()
    wv_hbm = nc.dram_tensor("wv", [D, 2 * D], F32, kind="ExternalInput").ap()
    wg_hbm = nc.dram_tensor("wg", [D, 2 * D], F32, kind="ExternalInput").ap()
    wo_hbm = nc.dram_tensor("wo", [2 * D, D], F32, kind="ExternalInput").ap()
    gnw_hbm = nc.dram_tensor("gnw", [1, DV], F32, kind="ExternalInput").ap()
    clipx_hbm = nc.dram_tensor("clipx", [1, 1], F32, kind="ExternalInput").ap()
    clipd_hbm = nc.dram_tensor("clipd", [1, 1], F32, kind="ExternalInput").ap()
    wgate_hbm = nc.dram_tensor("wgate", [DI, D], F32, kind="ExternalInput").ap()
    wup_hbm = nc.dram_tensor("wup", [DI, D], F32, kind="ExternalInput").ap()
    wdown_hbm = nc.dram_tensor("wdown", [D, DI], F32, kind="ExternalInput").ap()
    out_hbm = nc.dram_tensor("out", [NTOK, D], F32, kind="ExternalOutput").ap()

    xn_dram = nc.dram_tensor("xn_s", [NTOK, D], F32, kind=kind_s).ap()
    xnfm_dram = nc.dram_tensor("xnfm_s", [8, 128, NTOK], BF16, kind=kind_s).ap()
    qr_dram = nc.dram_tensor("qr_s", [NTOK, D], BF16, kind=kind_s).ap()
    kr_dram = nc.dram_tensor("kr_s", [NTOK, D], BF16, kind=kind_s).ap()
    xa_dram = nc.dram_tensor("xa_s", [NTOK, D], F32, kind=kind_s).ap()
    h_dram = nc.dram_tensor("h_s", [NIB, 128, NTOK], F32, kind=kind_s).ap()
    cc1_in = nc.dram_tensor("cc1_in", [1, 1], F32, kind="Internal").ap()
    cc1_out = nc.dram_tensor("cc1_out", [1, 1], F32, kind="Internal").ap()
    cc2_in = nc.dram_tensor("cc2_in", [1, 1], F32, kind="Internal").ap()
    cc2_out = nc.dram_tensor("cc2_out", [1, 1], F32, kind="Internal").ap()
    amx_dram = nc.dram_tensor("amx_s", [128, 1], F32, kind="Internal").ap()
    swd_dram = nc.dram_tensor("swd_s", [128, 8], F32, kind="Internal").ap()
    if dbg:
        dbg_qfm = nc.dram_tensor("dbg_qfm", [64, 2048], BF16,
                                 kind="ExternalOutput").ap()
        dbg_kfm = nc.dram_tensor("dbg_kfm", [64, 2048], BF16,
                                 kind="ExternalOutput").ap()
        dbg_qdq = nc.dram_tensor("dbg_qdq", [64, 2048], BF16,
                                 kind="ExternalOutput").ap()
        dbg_adt = nc.dram_tensor("dbg_adt", [128, 2048], BF16,
                                 kind="ExternalOutput").ap()
        dbg_osb = nc.dram_tensor("dbg_osb", [128, 2048], BF16,
                                 kind="ExternalOutput").ap()
        dbg_z = nc.dram_tensor("dbg_z", [128, 2048], BF16,
                               kind="ExternalOutput").ap()
        dbg_sf = nc.dram_tensor("dbg_sf", [64, 2048], F32,
                                kind="ExternalOutput").ap()
        dbg_f16 = nc.dram_tensor("dbg_f16", [128, 16], F32,
                                 kind="ExternalOutput").ap()
        dbg_pq = nc.dram_tensor("dbg_pq", [64, 512], F32,
                                kind="ExternalOutput").ap()
        dbg_kdkv = nc.dram_tensor("dbg_kdkv", [128, 1024], BF16,
                                  kind="ExternalOutput").ap()

    ident_h = nc.inline_tensor(tabs["ident"], "ident_t").ap()
    ct2_h = nc.inline_tensor(tabs["ct2"], "ct2_t").ap()
    st2_h = nc.inline_tensor(tabs["st2"], "st2_t").ap()

    guards = []

    def sb(name, shape, dtype):
        g = nc.sbuf_tensor(name, shape, dtype)
        h = g.__enter__()
        guards.append(g)
        return h

    def psum(name, shape, dtype=F32):
        g = nc.psum_tensor(name, shape, dtype)
        h = g.__enter__()
        guards.append(g)
        return h

    ident = sb("identB", [128, 128], BF16)
    ct2_sb = sb("ct2s", [128, nch * 64], BF16)
    st2_sb = sb("st2s", [128, nch * 64], BF16)

    dqfm_h = nc.inline_tensor(tabs["dqfm"], "dqfm_t").ap()
    dkv_h = nc.inline_tensor(tabs["dkv"], "dkv_t").ap()
    dt_h = nc.inline_tensor(tabs["dt"], "dt_t").ap()
    ones_h = nc.inline_tensor(tabs["ones_col"], "ones_t").ap()
    gC = tabs["gC"]
    dqfm = sb("dqfm", [64, 2048], BF16)
    dkv = sb("dkv", [128, 16], BF16)
    dtt = sb("dtt", [128, 2048], BF16)
    ones1 = sb("ones1", [1, 128], F32)
    gnw_sb = sb("gnw_sb", [1, 128], F32)
    gnw_col = sb("gnw_col", [128, 1], F32)
    warena = sb("warena", [128, 49152], BF16)
    wq_sb = warena[:, 0:8 * D]
    wk_sb = warena[:, 8 * D:16 * D]
    xb = sb("xb", [128, 4 * D], F32)
    xn_st = sb("xn_st", [128, 2 * D], F32)
    xnbf = sb("xnbf", [128, 4 * D], BF16)
    xnfm_sb = sb("xnfm_sb", [128, 4096], BF16)
    qr_sb = sb("qr_sb", [128, 4 * D], BF16)
    kr_sb = sb("kr_sb", [128, 4 * D], BF16)
    rope4 = sb("rope4", [128, 1024], BF16)
    sums = sb("sums", [128, 4], F32)
    sqs = sb("sqs", [128, 4], F32)
    sqdump = sb("sqdump", [128, 128], BF16)
    st_m = sb("st_m", [128, 4], F32)
    st_v = sb("st_v", [128, 4], F32)
    st_y = sb("st_y", [128, 4], F32)
    st_t = sb("st_t", [128, 4], F32)
    st_i1 = sb("st_i1", [128, 4], I32)
    st_i2 = sb("st_i2", [128, 4], I32)
    ln_r = sb("ln_r", [128, 4], F32)
    ln_mr = sb("ln_mr", [128, 4], F32)
    am_cols = sb("am_cols", [128, max(NTOK // 128, 1)], F32)
    am1 = sb("am1", [128, 1], F32)
    am1g = sb("am1g", [128, 1], F32)
    amrow = None
    gmax_x = sb("gmax_x", [128, 1], F32)
    gmax_h = sb("gmax_h", [128, 1], F32)
    clipx_sb = sb("clipx_sb", [1, 1], F32)
    clipd_sb = sb("clipd_sb", [1, 1], F32)
    sgm_x = sb("sgm_x", [128, 1], F32)
    sgm_d = sb("sgm_d", [128, 1], F32)
    sx_t = sb("sx_t", [128, 1], F32)
    rs_x = sb("rs_x", [128, 1], F32)
    sh_t = sb("sh_t", [128, 1], F32)
    wrw = sb("wrw", [128, 1], F32)
    rsh_t = wrw
    swg_tab = sb("swg_tab", [128, NIB], F32)
    swu_tab = sb("swu_tab", [128, NIB], F32)
    sg_tab = swg_tab
    su_tab = swu_tab
    rsq_tab = su_tab
    swd_tab = sb("swd_tab", [128, 8], F32)
    hacc = [sb("hacc0", [128, NIB], F32), sb("hacc1", [128, NIB], F32)]
    hm1 = None


    ldbuf = xnfm_sb
    zbuf = sb("zbuf", [128, 4096], BF16)
    S_f = [sb("S_f0", [64, 2048], F32), sb("S_f1", [64, 2048], F32)]
    S_b = [sb("S_b0", [64, 2048], BF16), sb("S_b1", [64, 2048], BF16)]
    qdq64 = sb("qdq64", [64, 2048], BF16)
    ss16 = sb("ss16", [128, 16], F32)
    f16 = sb("f16", [128, 16], F32)
    rs_i1 = sb("rs_i1", [128, 16], I32)
    rs_i2 = sb("rs_i2", [128, 16], I32)
    rs_y = sb("rs_y", [128, 16], F32)
    rs_t = sb("rs_t", [128, 16], F32)

    pT0 = psum("pT0", [128, 1024], BF16)
    pT1 = psum("pT1", [128, 1024], BF16)
    pQ0 = psum("pQ0", [128, 512])
    pQ1 = psum("pQ1", [128, 512])
    pK0 = psum("pK0", [128, 512])
    pK1 = psum("pK1", [128, 512])
    pR0 = psum("pR0", [128, 512])
    pR1 = psum("pR1", [128, 512])
    TRQ, TRK, TRZa, TRZb = pT0, pT1, pT0, pT1

    p = Plan(nc)
    p.sem("gg")
    for s in ("ld_tab", "gld_w", "ld_x", "st_xn", "st_qk", "st_fm",
              "ld_o", "st_o", "fv", "fs", "tg", "vg", "sg", "gld_w2",
              "ld2x_0", "ld2x_1", "ld2fm_0", "ld2fm_1",
              "ld2q_0", "ld2k_0",
              "st_xa0", "st_xa1", "st_xn0", "st_xn1"):
        p.sem(s)

    for ap_sb, ap_h in ((ident, ident_h),
                        (ct2_sb, ct2_h), (st2_sb, st2_h), (dqfm, dqfm_h),
                        (dkv, dkv_h), (dtt, dt_h), (ones1, ones_h),
                        (gnw_sb, gnw_hbm), (clipx_sb, clipx_hbm),
                        (clipd_sb, clipd_hbm)):
        p.dma("sync", ap_sb[:], ap_h[:], "ld_tab")
    tok_tab = ("ld_tab", p.counts["ld_tab"])
    for c in range(16):
        p.dma("gpsimd", warena[:, 16 * D + c * D:16 * D + (c + 1) * D],
              wo_hbm[c * 128:(c + 1) * 128, :], "gld_w2")
    tok_wo_raw = ("gld_w2", p.counts["gld_w2"])
    for c in range(8):
        p.dma("gpsimd", wq_sb[:, c * D:(c + 1) * D],
              wq_hbm[c * 128:(c + 1) * 128, :], "gld_w")
        p.dma("gpsimd", wk_sb[:, c * D:(c + 1) * D],
              wk_hbm[c * 128:(c + 1) * 128, :], "gld_w")
    tok_wqk = ("gld_w", p.counts["gld_w"])

    p.wait("vector", tok_tab)
    p.wait("tensor", tok_tab)
    p.wait("tensor", tok_wqk)

    def rope_half(bank, dst512, cblk, last_sig=None):
        ct = ct2_sb[:, cblk * 64:(cblk + 1) * 64].rearrange(
            "p (o j) -> p o j", o=1).broadcast_to([128, 8, 64])
        sl = st2_sb[:, cblk * 64:cblk * 64 + 32].rearrange(
            "p (o j) -> p o j", o=1).broadcast_to([128, 8, 32])
        sh = st2_sb[:, cblk * 64 + 32:(cblk + 1) * 64].rearrange(
            "p (o j) -> p o j", o=1).broadcast_to([128, 8, 32])
        sv = bank[:].rearrange("p (h sj) -> p h sj", sj=64)
        x1, x2 = sv[:, :, 0:32], sv[:, :, 32:64]
        rA_f = rope4[:, 0:512]
        rB_f = rope4[:, 512:1024]
        rAh = rA_f.rearrange("p (h j) -> p h j", j=64)
        rBh = rB_f.rearrange("p (h j) -> p h j", j=64)
        p.op("vector", lambda v: v.tensor_tensor(rAh, sv, ct, ALU.mult))
        p.op("vector", lambda v: v.tensor_tensor(
            rBh[:, :, 0:32], x2, sl, ALU.mult))
        tok = p.op("vector", lambda v: v.tensor_tensor(
            rBh[:, :, 32:64], x1, sh, ALU.mult), sig="fv")
        p.wait("vector", tok)
        tok = p.op("vector", lambda v: v.tensor_tensor(
            dst512, rA_f, rB_f, ALU.add), sig=last_sig or "fv")
        if last_sig is None:
            p.wait("vector", tok)
        return tok

    trev_tok = [None, None]
    xnrot_cast = [None, None]
    xnrot_st = [None, None]
    b1s = {"ln_last": None,
           "norm_last": None,
           "stqk": None, "stfm": None,
           "lntok": {},
           "casts": {}}

    def b1_A1(t):
        t0 = t * 512
        lw = [b1s["norm_last"]] if b1s["norm_last"] else []
        for s4 in range(4):
            p.dma("sync", xb[:, s4 * D:(s4 + 1) * D],
                  x_hbm[t0 + s4 * 128: t0 + (s4 + 1) * 128, :],
                  "ld_x", waits=lw)
            lw = []
        xall = ("ld_x", p.counts["ld_x"])
        if b1s["ln_last"]:
            p.wait("vector", b1s["ln_last"])
            p.wait("scalar", b1s["ln_last"])
        p.wait("vector", xall)
        p.wait("scalar", xall)
        for s4 in range(4):
            p.op("vector", lambda v, s4=s4: v.reduce_sum(
                sums[:, s4:s4 + 1], xb[:, s4 * D:(s4 + 1) * D], AX),
                sig="vg" if s4 == 3 else None)
            p.op("scalar", lambda s, s4=s4: s.activation(
                zbuf[:, s4 * D:(s4 + 1) * D], xb[:, s4 * D:(s4 + 1) * D],
                ACT.Square, accum_out=sqs[:, s4:s4 + 1]),
                sig="sg" if s4 == 3 else None)
        sq_tok = ("sg", p.counts["sg"])

        def fv(build):
            tk = p.op("vector", build, sig="fv")
            p.wait("vector", tk)
        p.wait("vector", sq_tok)
        p.wait("vector", ("vg", p.counts["vg"]))
        if b1s["norm_last"]:
            p.wait("vector", b1s["norm_last"])
        fv(lambda v: v.tensor_scalar(st_m[:], sums[:], 1.0 / D, None,
                                     ALU.mult))
        fv(lambda v: v.tensor_tensor(st_v[:], st_m[:], st_m[:], ALU.mult))
        fv(lambda v: v.scalar_tensor_tensor(
            st_v[:], sqs[:], 1.0 / D, st_v[:], ALU.mult, ALU.subtract))
        fv(lambda v: v.tensor_scalar(st_v[:], st_v[:], 1e-5, None, ALU.add))
        emit_rsqrt(p, st_i1[:], st_i2[:], st_v[:], ln_r[:], st_y[:], st_t[:])
        fv(lambda v: v.tensor_tensor(ln_mr[:], st_m[:], ln_r[:], ALU.mult))
        tok = p.op("vector", lambda v: v.tensor_scalar(
            ln_mr[:], ln_mr[:], -1.0, None, ALU.mult), sig="vg")
        b1s["ln_last"] = tok
        b1s["lntok"][t] = tok

    def b1_A2_s4(t, s4, extra_war=None):
        g = t * 4 + s4
        rot = s4 % 2
        xn_ap = xn_st[:, rot * D:(rot + 1) * D]
        wts = [tk for tk in (xnrot_cast[rot], xnrot_st[rot]) if tk]
        p.waits("scalar", wts)
        p.wait("scalar", b1s["lntok"][t])
        tok = p.op("scalar", lambda s, s4=s4, t=t, xn_ap=xn_ap: s.activation(
            xn_ap, xb[:, s4 * D:(s4 + 1) * D], ACT.Identity,
            bias=ln_mr[:, s4:s4 + 1], scale=ln_r[:, s4:s4 + 1]), sig="sg")
        b1s["norm_last"] = tok
        p.wait("vector", tok)
        p.wait("gpsimd", tok)
        p.op("vector", lambda v, g=g, xn_ap=xn_ap: v.reduce_max(
            am_cols[:, g:g + 1], xn_ap, AX, apply_absolute_value=True))
        p.dma("gpsimd", xn_dram[g * 128:(g + 1) * 128, :], xn_ap,
              "st_xn%d" % rot)
        xnrot_st[rot] = ("st_xn%d" % rot, p.counts["st_xn%d" % rot])
        if extra_war:
            p.wait("vector", extra_war)
        tok = p.op("vector", lambda v, s4=s4, xn_ap=xn_ap: v.tensor_copy(
            xnbf[:, s4 * D:(s4 + 1) * D], xn_ap), sig="vg")
        xnrot_cast[rot] = tok
        b1s["casts"].setdefault(t, {})[s4] = tok

    def b1_B(t):
        t0 = t * 512
        g0 = t * 4
        p.waits("vector", [tk for tk in (b1s["stqk"], b1s["stfm"]) if tk])
        if b1s["stfm"]:
            p.wait("scalar", b1s["stfm"])
        for s4 in range(4):
            g = g0 + s4
            cblk = g % nch
            xnb_ap = xnbf[:, s4 * D:(s4 + 1) * D]
            bank = pT0 if s4 % 2 == 0 else pT1
            if trev_tok[s4 % 2] is not None:
                p.wait("tensor", trev_tok[s4 % 2])
            p.wait("tensor", b1s["casts"][t][s4])
            for c in range(8):
                p.op("tensor", lambda t_, c=c, bank=bank, xnb_ap=xnb_ap:
                     t_.transpose(bank[:, c * 128:(c + 1) * 128],
                                  xnb_ap[:, c * 128:(c + 1) * 128],
                                  ident[:]), sig="tg" if c == 7 else None)
            ttok = ("tg", p.counts["tg"])
            p.wait("scalar", ttok)
            dst = xnfm_sb[:, 0:4096].rearrange(
                "p (c s f) -> p c s f", s=4, f=128)[:, :, s4, :]
            etok = p.op("scalar", lambda s, bank=bank, dst=dst: s.activation(
                dst, bank[:].rearrange("p (c f) -> p c f", f=128), ACT.Copy),
                sig="sg")
            trev_tok[s4 % 2] = etok
            if t + 1 < NT512:
                b1_A2_s4(t + 1, s4, extra_war=ttok)
            p.wait("tensor", etok)
            if b1s.get("rope_prev"):
                p.wait("tensor", b1s["rope_prev"])
            for c in range(8):
                lhs = xnfm_sb[:, c * 512 + s4 * 128: c * 512 + s4 * 128 + 128]
                p.op("tensor", lambda t_, lhs=lhs: t_.ldweights(lhs))
                for bi, (bank2, w_sb, off) in enumerate((
                        (pQ0, wq_sb, 0), (pQ1, wq_sb, 512),
                        (pK0, wk_sb, 0), (pK1, wk_sb, 512))):
                    rhs = w_sb[:, c * D + off: c * D + off + 512]
                    p.op("tensor", lambda t_, bank2=bank2, lhs=lhs, rhs=rhs,
                         c=c: t_.matmul(bank2[:], lhs, rhs,
                                        start=(c == 0), stop=(c == 7)),
                         sig="tg" if (c == 7 and bi == 3) else None)
            mtok = ("tg", p.counts["tg"])
            p.wait("vector", mtok)
            qdst = qr_sb[:, s4 * D:(s4 + 1) * D]
            kdst = kr_sb[:, s4 * D:(s4 + 1) * D]
            rope_half(pQ0, qdst[:, 0:512], cblk)
            rope_half(pQ1, qdst[:, 512:1024], cblk)
            rope_half(pK0, kdst[:, 0:512], cblk)
            rtok = rope_half(pK1, kdst[:, 512:1024], cblk, last_sig="vg")
            b1s["rope_prev"] = rtok
            p.wait("gpsimd", rtok)
            p.dma("gpsimd", qr_dram[g * 128:(g + 1) * 128, :], qdst, "st_qk")
            p.dma("gpsimd", kr_dram[g * 128:(g + 1) * 128, :], kdst, "st_qk")
        p.wait("gpsimd", ("vg", p.counts["vg"]))
        for c in range(8):
            p.dma("gpsimd", xnfm_dram[c, :, t0:t0 + 512],
                  xnfm_sb[:, c * 512:(c + 1) * 512], "st_fm")
        b1s["stqk"] = ("st_qk", p.counts["st_qk"])
        b1s["stfm"] = ("st_fm", p.counts["st_fm"])

    b1_A1(0)
    for s4 in range(4):
        b1_A2_s4(0, s4)
    for tt in range(NT512):
        if tt + 1 < NT512:
            b1_A1(tt + 1)
        b1_B(tt)

    amrow = sqdump[0:1, 0:256].bitcast(F32) if False else xn_st[0:1, 0:128]
    ams = xn_st[0:1, 128:129]
    for s in ("ccs", "gz"):
        p.sem(s)
    p.wait("vector", ("sg", p.counts["sg"]))
    tok = p.op("vector", lambda v: v.reduce_max(
        am1[:], am_cols[:], AX, apply_absolute_value=False), sig="vg")
    p.wait("gpsimd", tok)
    t1 = p.dma("gpsimd", amx_dram[:], am1[:], "st_o")
    t2 = p.dma("gpsimd", amrow,
               amx_dram[:].rearrange("(o p) c -> o (p c)", o=1), "gld_w2",
               waits=[t1, ("st_xn0", p.counts["st_xn0"]),
                      ("st_xn1", p.counts["st_xn1"])])
    p.wait("vector", t2)
    tok = p.op("vector", lambda v: v.reduce_max(ams, amrow, AX),
               sig="vg")
    p.wait("gpsimd", tok)
    t1 = p.dma("gpsimd", cc1_in[:], ams, "st_o")
    p.wait("gpsimd", t1)
    x1_cc = p.op("gpsimd", lambda g: g.collective_compute(
        "AllReduce", ALU.max, replica_groups=[list(range(8))],
        ins=[cc1_in[:]], outs=[cc1_out[:]]), sig="ccs")

    NCHG = nb * nch
    barrier = [("tg", p.counts["tg"]), ("vg", p.counts["vg"]),
               ("sg", p.counts["sg"]),
               ("st_fm", p.counts["st_fm"]), ("st_qk", p.counts["st_qk"]),
               ("st_xn0", p.counts["st_xn0"]), ("st_xn1", p.counts["st_xn1"])]
    for e in ENGINES:
        p.waits(e, barrier)

    wv_v = lambda c: warena[:, c * 2048:(c + 1) * 2048]
    wg_v = lambda c: warena[:, 16 * D + c * 2048: 16 * D + (c + 1) * 2048]
    woraw_v = lambda c: warena[:, 16 * D + c * D: 16 * D + (c + 1) * D]
    wo_v = lambda c: warena[:, 32 * D + c * D: 32 * D + (c + 1) * D]
    x_tok = lambda q: xb[:, q * D:(q + 1) * D]
    xa_v = lambda q: xn_st[:, q * D:(q + 1) * D]
    v_tok = xnbf[:, 0:2048]
    gsil = xnbf[:, 2048:4096]
    xnfm_ch = lambda q: ldbuf[:, q * 1024:(q + 1) * 1024]
    qr_ch = lambda q: ldbuf[:, 2048:3072]
    kr_ch = lambda q: ldbuf[:, 3072:4096]
    qfm = qr_sb[0:64, 0:2048]
    kfm = qr_sb[0:64, 2048:4096]
    kdkv = rope4[:, 0:1024]
    adt = kr_sb[:, 0:2048]
    o_sb = kr_sb[:, 2048:4096]
    z_tok = zbuf[:, 0:2048]
    zfm = zbuf[:, 2048:4096]
    sqd2 = sqdump[:, 0:128]

    B2P = int(os.environ.get("B2_PRO", "1"))
    p.wait("tensor", tok_tab)
    if B2P:
        tok = p.op("tensor", lambda t: t.matmul(
            pR0[:, 0:128], gnw_sb[:], ones1[:], start=True, stop=True),
            sig="tg")
        p.wait("vector", tok)
        tok = p.op("vector", lambda v: v.tensor_copy(gnw_col[:], pR0[:, 0:1]),
                   sig="fv")
        p.wait("vector", tok)
        p.wait("vector", tok_wo_raw)
        for c in range(16):
            p.op("vector", lambda v, c=c: v.tensor_scalar(
                wo_v(c), woraw_v(c), gnw_col[:], None, ALU.mult),
                sig="vg" if c == 15 else None)
    else:
        p.wait("vector", tok_wo_raw)
        p.op("vector", lambda v: v.tensor_copy(wo_v(0), woraw_v(0)),
             sig="vg")
    tok_wo = ("vg", p.counts["vg"])
    p.wait("gpsimd", tok_wo)
    for c in range(8):
        p.dma("gpsimd", wv_v(c), wv_hbm[c * 128:(c + 1) * 128, :], "gld_w2")
        p.dma("gpsimd", wg_v(c), wg_hbm[c * 128:(c + 1) * 128, :], "gld_w2")
    tok_wvg = ("gld_w2", p.counts["gld_w2"])
    p.wait("tensor", tok_wvg)

    fr_x = [None, None]
    fr_fm = [None, None]
    fr_q = [None, None]
    fr_k = [None, None]
    fr_xa = [None, None]
    fr_vg = None
    fr_qkfm = None
    fr_adt = None
    fr_z = None
    fr_zfm = None
    fr_T = None
    fr_osb = None
    fr_scast = [None, None]
    fr_sstt = [None, None]
    fr_ss = None

    for b in range(nb):
        pz = 0
        p.waits("vector", [t for t in (fr_scast[pz], fr_sstt[pz]) if t])
        tok = p.op("vector", lambda v: v.memset(S_f[pz][:], 0.0), sig="fv")
        p.wait("vector", tok)
        tok = p.op("vector", lambda v: v.memset(S_b[pz][:], 0.0), sig="vg")
        fr_scast[pz] = tok
        p.wait("tensor", tok)
        B2S = int(os.environ.get("B2_STAGE", "9"))
        for ci in range(nch):
            kg = b * nch + ci
            q2 = kg % 2
            row0 = (b * T + ci * 128)
            s_in = ci % 2
            s_out = (ci + 1) % 2
            tx = p.dma("sync", x_tok(q2), x_hbm[row0:row0 + 128, :],
                       "ld2x_%d" % q2,
                       waits=[t for t in (fr_x[q2],) if t])
            tfm = p.dma("sync",
                        xnfm_ch(q2).rearrange("p (c t) -> p c t", t=128),
                        xnfm_dram[:, :, row0:row0 + 128].transpose([1, 0, 2]),
                        "ld2fm_%d" % q2,
                        waits=[t for t in (fr_fm[q2],) if t])
            tq = p.dma("sync", qr_ch(q2), qr_dram[row0:row0 + 128, :],
                       "ld2q_0",
                       waits=[t for t in (fr_q[0],) if t])
            tk = p.dma("sync", kr_ch(q2), kr_dram[row0:row0 + 128, :],
                       "ld2k_0",
                       waits=[t for t in (fr_k[0], fr_k[1]) if t])
            if B2S < 1:
                _fake_resid(p, q2, tx, fr_x, fr_xa, xa_v, x_tok, xa_dram, row0)
                fr_fm[q2] = tfm
                fr_q[0] = tq
                fr_k[0] = tk
                continue
            p.wait("tensor", tfm)
            if fr_vg:
                p.wait("tensor", fr_vg)
            w_banks = (pQ0, pQ1, pK0, pK1)
            for c in range(8):
                lhs = xnfm_ch(q2)[:, c * 128:(c + 1) * 128]
                p.op("tensor", lambda t, lhs=lhs: t.ldweights(lhs))
                for bi in range(4):
                    w_ap = wv_v(c)[:, bi * 512:(bi + 1) * 512]
                    p.op("tensor", lambda t, bi=bi, lhs=lhs, w_ap=w_ap, c=c:
                         t.matmul(w_banks[bi][:], lhs, w_ap,
                                  start=(c == 0), stop=(c == 7)),
                         sig="tg" if (c == 7 and bi == 3) else None)
            t_v = ("tg", p.counts["tg"])
            p.wait("vector", t_v)
            p.wait("scalar", t_v)
            ev_v = []
            for bi, bk in enumerate(w_banks):
                if bi < 2:
                    ev_v.append(p.op("vector", lambda v, bi=bi, bk=bk:
                                     v.tensor_copy(
                                         v_tok[:, bi * 512:(bi + 1) * 512],
                                         bk[:]), sig="vg"))
                else:
                    ev_v.append(p.op("scalar", lambda s, bi=bi, bk=bk:
                                     s.activation(
                                         v_tok[:, bi * 512:(bi + 1) * 512],
                                         bk[:], ACT.Copy), sig="sg"))
            p.waits("tensor", ev_v)
            for c in range(8):
                lhs = xnfm_ch(q2)[:, c * 128:(c + 1) * 128]
                p.op("tensor", lambda t, lhs=lhs: t.ldweights(lhs))
                for bi in range(4):
                    w_ap = wg_v(c)[:, bi * 512:(bi + 1) * 512]
                    p.op("tensor", lambda t, bi=bi, lhs=lhs, w_ap=w_ap, c=c:
                         t.matmul(w_banks[bi][:], lhs, w_ap,
                                  start=(c == 0), stop=(c == 7)),
                         sig="tg" if (c == 7 and bi == 3) else None)
            t_g = ("tg", p.counts["tg"])
            p.wait("scalar", t_g)
            ev_g = []
            for bi, bk in enumerate(w_banks):
                ev_g.append(p.op("scalar", lambda s, bi=bi, bk=bk:
                                 s.activation(
                                     gsil[:, bi * 512:(bi + 1) * 512], bk[:],
                                     ACT.Silu), sig="sg"))
            if B2S < 2:
                p.waits("tensor", ev_g)
                _fake_resid(p, q2, tx, fr_x, fr_xa, xa_v, x_tok, xa_dram, row0)
                fr_vg = fr_x[q2]
                fr_fm[q2] = t_g
                fr_q[0] = tq
                fr_k[0] = tk
                continue
            if fr_qkfm:
                p.wait("tensor", fr_qkfm)
            if fr_T:
                p.wait("tensor", fr_T)
            p.wait("tensor", tq)
            qsrc = qr_ch(q2)
            for h in range(H):
                bank = TRQ if h < 8 else TRK
                p.op("tensor", lambda t, h=h, bank=bank, qsrc=qsrc:
                     t.transpose(bank[0:64, (h % 8) * 128:(h % 8 + 1) * 128],
                                 qsrc[:, h * 64:(h + 1) * 64], ident[:]),
                    sig="tg" if h == H - 1 else None)
            fr_q[0] = ("tg", p.counts["tg"])
            p.wait("vector", fr_q[0])
            p.op("vector", lambda v: v.tensor_copy(qfm[:, 0:1024],
                                                   TRQ[0:64, :]))
            tok = p.op("vector", lambda v: v.tensor_copy(
                qfm[:, 1024:2048], TRK[0:64, :]), sig="vg")
            t_qcp = tok
            p.wait("vector", t_qcp)
            tok = p.op("vector", lambda v: v.tensor_tensor(
                qdq64[:], qfm[:], dqfm[:], ALU.mult), sig="vg")
            t_qfm_ev = tok
            p.wait("tensor", tk)
            p.wait("tensor", t_qcp)
            ksrc = kr_ch(q2)
            for h in range(H):
                bank = TRQ if h < 8 else TRK
                p.op("tensor", lambda t, h=h, bank=bank, ksrc=ksrc:
                     t.transpose(bank[0:64, (h % 8) * 128:(h % 8 + 1) * 128],
                                 ksrc[:, h * 64:(h + 1) * 64], ident[:]),
                    sig="tg" if h == H - 1 else None)
            t_ktr = ("tg", p.counts["tg"])
            p.wait("vector", t_ktr)
            p.op("vector", lambda v: v.tensor_copy(kfm[:, 0:1024],
                                                   TRQ[0:64, :]))
            tok = p.op("vector", lambda v: v.tensor_copy(
                kfm[:, 1024:2048], TRK[0:64, :]), sig="vg")
            t_kfm = tok
            p.wait("vector", tk)
            tok = p.op("vector", lambda v, q2=q2: v.tensor_tensor(
                kdkv.rearrange("p (h dk) -> p h dk", dk=64),
                kr_ch(q2)[:].rearrange("p (h dk) -> p h dk", dk=64),
                dkv[:].rearrange("p (h o) -> p h o", o=1).broadcast_to(
                    [128, 16, 64]), ALU.mult), sig="vg")
            t_qk_ev = tok
            fr_k[0] = t_ktr
            fr_k[1] = tok
            if B2S < 3:
                p.waits("tensor", ev_g)
                _fake_resid(p, q2, tx, fr_x, fr_xa, xa_v, x_tok, xa_dram, row0)
                fr_vg = fr_x[q2]
                fr_fm[q2] = t_g
                fr_qkfm = ("tg", p.counts["tg"])
                continue
            p.wait("tensor", t_qk_ev)
            p.waits("tensor", ev_g)
            if fr_zfm:
                p.wait("scalar", fr_zfm)
            adt_toks = [None] * 4
            sq_toks = []
            ocp_toks = []
            stt_toks = []
            for gi in range(4):
                bkA = pR0 if gi % 2 == 0 else pK1
                bkO = pR1 if gi % 2 == 0 else pK0
                bkS = pQ0 if gi % 2 == 0 else pQ1
                if gi >= 2:
                    p.wait("tensor", adt_toks[gi - 2])
                    if len(ocp_toks) > gi - 2:
                        p.wait("tensor", ocp_toks[gi - 2])
                        p.wait("tensor", sq_toks[gi - 2])
                    if len(stt_toks) > gi - 2:
                        p.wait("tensor", stt_toks[gi - 2])
                for hh in range(4):
                    h = gi * 4 + hh
                    lhs = kfm[:, h * 128:(h + 1) * 128]
                    rhs = qfm[:, h * 128:(h + 1) * 128]
                    p.op("tensor", lambda t, hh=hh, lhs=lhs, rhs=rhs, bkA=bkA:
                         t.matmul(bkA[:, hh * 128:(hh + 1) * 128], lhs, rhs,
                                  start=True, stop=True),
                         sig="tg" if hh == 3 else None)
                t_A = ("tg", p.counts["tg"])
                p.wait("vector", t_A)
                atok = p.op("vector", lambda v, gi=gi, bkA=bkA: v.tensor_tensor(
                    adt[:, gi * 512:(gi + 1) * 512], bkA[:],
                    dtt[:, gi * 512:(gi + 1) * 512], ALU.mult), sig="vg")
                adt_toks[gi] = atok
                if B2S < 4:
                    p.wait("tensor", adt_toks[gi])
                    continue
                p.wait("tensor", atok)
                for hh in range(4):
                    h = gi * 4 + hh
                    lhs1 = adt[:, h * 128:(h + 1) * 128]
                    rhs1 = v_tok[:, h * 128:(h + 1) * 128]
                    lhs2 = qdq64[:, h * 128:(h + 1) * 128]
                    rhs2 = S_b[s_in][:, h * 128:(h + 1) * 128]
                    p.op("tensor", lambda t, hh=hh, lhs1=lhs1, rhs1=rhs1,
                         bkO=bkO: t.matmul(bkO[:, hh * 128:(hh + 1) * 128],
                                           lhs1, rhs1, start=True, stop=False))
                    p.op("tensor", lambda t, hh=hh, lhs2=lhs2, rhs2=rhs2,
                         bkO=bkO: t.matmul(bkO[:, hh * 128:(hh + 1) * 128],
                                           lhs2, rhs2, start=False, stop=True),
                         sig="tg" if hh == 3 else None)
                t_o = ("tg", p.counts["tg"])
                p.wait("scalar", t_o)
                if fr_ss:
                    p.wait("scalar", fr_ss)
                for hh in range(4):
                    h = gi * 4 + hh
                    tok = p.op("scalar", lambda s, hh=hh, h=h, bkO=bkO:
                               s.activation(
                                   zfm[:, h * 128:(h + 1) * 128],
                                   bkO[:, hh * 128:(hh + 1) * 128],
                                   ACT.Square, accum_out=ss16[:, h:h + 1]),
                               sig="sg" if hh == 3 else None)
                sq_toks.append(tok)
                tok = p.op("scalar", lambda s, gi=gi, bkO=bkO: s.activation(
                    o_sb[:, gi * 512:(gi + 1) * 512], bkO[:], ACT.Copy),
                    sig="sg")
                ocp_toks.append(tok)
                if B2S < 5:
                    p.wait("tensor", adt_toks[gi])
                    p.wait("tensor", ocp_toks[gi])
                    continue
                for hh in range(4):
                    h = gi * 4 + hh
                    lhs = kdkv[:, h * 64:(h + 1) * 64]
                    rhs = v_tok[:, h * 128:(h + 1) * 128]
                    p.op("tensor", lambda t, hh=hh, lhs=lhs, rhs=rhs, bkS=bkS:
                         t.matmul(bkS[0:64, hh * 128:(hh + 1) * 128],
                                  lhs, rhs, start=True, stop=True),
                         sig="tg" if hh == 3 else None)
                t_s = ("tg", p.counts["tg"])
                p.wait("vector", t_s)
                if dbg and kg == 0 and gi == 0:
                    ptok = p.op("vector", lambda v: v.tensor_copy(
                        xn_st[0:64, 0:512], pQ0[0:64, :]), sig="vg")
                    p.wait("gpsimd", ptok)
                    p.dma("gpsimd", dbg_pq[:], xn_st[0:64, 0:512], "st_o")
                    p.dma("gpsimd", dbg_kdkv[:], kdkv, "st_o")
                    dbgtok = ("st_o", p.counts["st_o"])
                    p.wait("vector", dbgtok)
                    p.wait("scalar", dbgtok)
                p.waits("vector", [t for t in (fr_scast[s_out],
                                               fr_sstt[s_out]) if t])
                for hh in range(4):
                    h = gi * 4 + hh
                    tok = p.op("vector", lambda v, hh=hh, h=h, bkS=bkS,
                               si=s_in, so=s_out: v.scalar_tensor_tensor(
                                   S_f[so][:, h * 128:(h + 1) * 128],
                                   S_f[si][:, h * 128:(h + 1) * 128],
                                   float(gC[h]),
                                   bkS[0:64, hh * 128:(hh + 1) * 128],
                                   ALU.mult, ALU.add),
                               sig="vg" if hh == 3 else None)
                stt_toks.append(("vg", p.counts["vg"]))
            for tk in (ocp_toks[-2:] + sq_toks[-2:]):
                p.wait("tensor", tk)
            fr_vg = ("tg", p.counts["tg"])
            fr_qkfm = fr_vg
            if B2S < 5:
                if B2S < 4:
                    sq_toks = []
                    ocp_toks = []
                p.waits("tensor", ev_g)
                p.waits("tensor", ocp_toks)
                _fake_resid(p, q2, tx, fr_x, fr_xa, xa_v, x_tok, xa_dram, row0)
                fr_vg = fr_x[q2]
                fr_fm[q2] = t_g
                continue
            p.waits("vector", stt_toks)
            tok = p.op("vector", lambda v, so=s_out: v.tensor_copy(
                S_b[so][:], S_f[so][:]), sig="vg")
            fr_scast[s_out] = tok
            fr_sstt[s_out] = stt_toks[-1]
            p.wait("tensor", tok)
            p.waits("vector", sq_toks)
            if fr_ss:
                p.wait("vector", fr_ss)
            tok = p.op("vector", lambda v: v.tensor_scalar(
                rs_t[:], ss16[:], 1.0 / DV, 1e-5, ALU.mult, ALU.add),
                sig="fv")
            p.wait("vector", tok)
            emit_rsqrt(p, rs_i1[:], rs_i2[:], rs_t[:], f16[:], rs_y[:],
                       ss16[:])
            fr_ss = ("fv", p.counts["fv"])
            if B2S < 6:
                p.waits("tensor", ev_g)
                _fake_resid(p, q2, tx, fr_x, fr_xa, xa_v, x_tok, xa_dram, row0)
                fr_vg = fr_x[q2]
                fr_fm[q2] = t_g
                continue
            p.waits("vector", ocp_toks)
            p.waits("vector", ev_g)
            if fr_z:
                p.wait("vector", fr_z)
            for h in range(H):
                p.op("vector", lambda v, h=h: v.scalar_tensor_tensor(
                    z_tok[:, h * 128:(h + 1) * 128],
                    o_sb[:, h * 128:(h + 1) * 128], f16[:, h:h + 1],
                    gsil[:, h * 128:(h + 1) * 128], ALU.mult, ALU.mult),
                    sig="vg" if h == H - 1 else None)
            t_z = ("vg", p.counts["vg"])
            fr_osb = t_z
            if B2S < 7:
                p.waits("tensor", ev_g)
                _fake_resid(p, q2, tx, fr_x, fr_xa, xa_v, x_tok, xa_dram, row0)
                fr_vg = fr_x[q2]
                fr_fm[q2] = t_g
                fr_z = t_z
                continue
            p.wait("tensor", t_z)
            p.wait("tensor", t_qfm_ev)
            p.wait("tensor", t_kfm)
            if fr_zfm:
                p.wait("tensor", fr_zfm)
            for h in range(8):
                p.op("tensor", lambda t, h=h: t.transpose(
                    TRZa[:, h * 128:(h + 1) * 128],
                    z_tok[:, h * 128:(h + 1) * 128], ident[:]),
                    sig="tg" if h == 7 else None)
            t_za = ("tg", p.counts["tg"])
            for h in range(8):
                p.op("tensor", lambda t, h=h: t.transpose(
                    TRZb[:, h * 128:(h + 1) * 128],
                    z_tok[:, 1024 + h * 128:1024 + (h + 1) * 128], ident[:]),
                    sig="tg" if h == 7 else None)
            t_zb = ("tg", p.counts["tg"])
            fr_z = t_zb
            p.wait("vector", t_za)
            p.op("vector", lambda v: v.tensor_copy(zfm[:, 0:1024], TRZa[:]))
            p.wait("vector", t_zb)
            tok = p.op("vector", lambda v: v.tensor_copy(
                zfm[:, 1024:2048], TRZb[:]), sig="vg")
            t_zfm = tok
            fr_T = tok
            if B2S < 8:
                p.waits("tensor", ev_g)
                _fake_resid(p, q2, tx, fr_x, fr_xa, xa_v, x_tok, xa_dram, row0)
                fr_vg = fr_x[q2]
                fr_fm[q2] = t_g
                fr_zfm = t_zfm
                continue
            p.wait("tensor", t_zfm)
            for h in range(16):
                lhs = zfm[:, h * 128:(h + 1) * 128]
                p.op("tensor", lambda t, lhs=lhs: t.ldweights(lhs))
                p.op("tensor", lambda t, h=h, lhs=lhs: t.matmul(
                    pQ1[:], lhs, wo_v(h)[:, 0:512],
                    start=(h == 0), stop=(h == 15)))
                p.op("tensor", lambda t, h=h, lhs=lhs: t.matmul(
                    pK0[:], lhs, wo_v(h)[:, 512:1024],
                    start=(h == 0), stop=(h == 15)),
                    sig="tg" if h == 15 else None)
            t_wo = ("tg", p.counts["tg"])
            fr_zfm = t_wo
            p.wait("vector", t_wo)
            p.wait("vector", tx)
            if fr_xa[q2]:
                p.wait("vector", fr_xa[q2])
            p.op("vector", lambda v, q2=q2: v.tensor_tensor(
                xa_v(q2)[:, 0:512], pQ1[:], x_tok(q2)[:, 0:512], ALU.add))
            tok = p.op("vector", lambda v, q2=q2: v.tensor_tensor(
                xa_v(q2)[:, 512:1024], pK0[:], x_tok(q2)[:, 512:1024],
                ALU.add), sig="vg")
            fr_x[q2] = tok
            p.wait("gpsimd", tok)
            p.dma("gpsimd", xa_dram[row0:row0 + 128, :], xa_v(q2),
                  "st_xa%d" % q2)
            fr_xa[q2] = ("st_xa%d" % q2, p.counts["st_xa%d" % q2])
            fr_fm[q2] = t_g
            fr_vg = fr_x[q2]
            if dbg and kg == 0:
                for dst, apx in ((dbg_qfm, qfm), (dbg_kfm, kfm),
                                 (dbg_qdq, qdq64[:]), (dbg_adt, adt),
                                 (dbg_osb, o_sb), (dbg_z, z_tok),
                                 (dbg_sf, S_f[s_out][:]),
                                 (dbg_f16, f16[:])):
                    p.dma("gpsimd", dst[:] if dst.shape else dst, apx, "st_o")
                for e in ENGINES:
                    p.wait(e, ("st_o", p.counts["st_o"]))

    barrier2 = [("vg", p.counts["vg"]), ("sg", p.counts["sg"]),
                ("tg", p.counts["tg"]),
                ("st_xa0", p.counts["st_xa0"]),
                ("st_xa1", p.counts["st_xa1"]),
                ("st_xn0", p.counts["st_xn0"]),
                ("st_xn1", p.counts["st_xn1"])]
    for e in ENGINES:
        p.waits(e, barrier2)
    for s in ("ld_wq0", "ld_wq1", "ld3x", "st_h0", "st_h1",
              "ld_h0", "ld_h1", "ld_h2", "ld_h3", "ld_xa", "ld_wd",
              "ld_sw"):
        p.sem(s)

    wgT_v = lambda pc: warena[:, pc * DI:(pc + 1) * DI]
    wuT_v = lambda pc: warena[:, 8 * DI + pc * DI: 8 * DI + (pc + 1) * DI]
    wqf_v = lambda q: xb[:, q * D:(q + 1) * D]
    xnC_v = lambda q: xb[:, q * D:(q + 1) * D]
    hbuf_v = lambda q: xb[:, 2048 + q * 512: 2048 + (q + 1) * 512]
    xq_v = lambda s4: xnbf[:, s4 * D:(s4 + 1) * D]
    sgate_v = lambda q: xn_st[:, q * 512:(q + 1) * 512]
    wint_v = lambda q: xnbf[:, q * D:(q + 1) * D]

    p.wait("tensor", ("gld_w2", p.counts["gld_w2"]))
    p.op("tensor", lambda t: t.matmul(pR0[:, 0:1], ones1[:], clipx_sb[:],
                                      start=True, stop=True))
    tok = p.op("tensor", lambda t: t.matmul(pR0[:, 128:129], ones1[:],
                                            clipd_sb[:], start=True,
                                            stop=True), sig="tg")
    p.wait("scalar", tok)
    p.op("scalar", lambda s: s.activation(sgm_x[:], pR0[:, 0:1], ACT.Sigmoid))
    tok = p.op("scalar", lambda s: s.activation(sgm_d[:], pR0[:, 128:129],
                                                ACT.Sigmoid), sig="sg")
    sgm_tok = tok
    p.wait("gpsimd", x1_cc)
    t2g = p.dma("gpsimd", ams, cc1_out[:], "gld_w2")
    p.wait("tensor", t2g)
    p.wait("tensor", sgm_tok)
    tok = p.op("tensor", lambda t: t.matmul(
        pR0[:, 256:257], ones1[:], ams, start=True, stop=True), sig="tg")
    p.wait("vector", tok)
    x1_tok = p.op("vector", lambda v: v.tensor_copy(gmax_x[:],
                                                    pR0[:, 256:257]),
                  sig="vg")
    p.wait("vector", sgm_tok)
    p.wait("vector", x1_tok)

    def fvv(build):
        tok = p.op("vector", build, sig="fv")
        p.wait("vector", tok)
    fvv(lambda v: v.tensor_tensor(sx_t[:], sgm_x[:], gmax_x[:], ALU.mult))
    fvv(lambda v: v.reciprocal(rs_x[:], sx_t[:]))
    fvv(lambda v: v.tensor_scalar(rs_x[:], rs_x[:], 127.0, None, ALU.mult))

    fr_wq = [None, None]
    wq_ev = None
    for wi, (w_hbm, wT_v, stab) in enumerate(
            ((wgate_hbm, wgT_v, swg_tab), (wup_hbm, wuT_v, swu_tab))):
        wbase = warena[:, wi * 8 * DI:(wi + 1) * 8 * DI]
        for ib in range(NIB):
            q = (wi * NIB + ib) % 2
            tok = p.dma("sync", wqf_v(q), w_hbm[ib * 128:(ib + 1) * 128, :],
                        "ld_wq%d" % q,
                        waits=[t for t in (fr_wq[q],) if t])
            p.wait("vector", tok)
            fvv(lambda v, q=q, stab=stab, ib=ib: v.reduce_max(
                stab[:, ib:ib + 1], wqf_v(q), AX, apply_absolute_value=True))
            fvv(lambda v, stab=stab, ib=ib: v.reciprocal(
                wrw[:], stab[:, ib:ib + 1]))
            fvv(lambda v: v.tensor_scalar(wrw[:], wrw[:], 127.0, None,
                                          ALU.mult))
            fvv(lambda v, q=q: v.tensor_scalar(
                wqf_v(q), wqf_v(q), wrw[:], RND, ALU.mult, ALU.add))
            tok = p.op("vector", lambda v, q=q: v.tensor_scalar(
                wint_v(q), wqf_v(q), RND, None, ALU.subtract), sig="vg")
            fr_wq[q] = tok
            p.wait("tensor", tok)
            if wq_ev:
                p.wait("tensor", wq_ev)
            for c in range(8):
                p.op("tensor", lambda t, c=c, q=q: t.transpose(
                    TRQ[:, c * 128:(c + 1) * 128],
                    wint_v(q)[:, c * 128:(c + 1) * 128], ident[:]),
                    sig="tg" if c == 7 else None)
            ttok = ("tg", p.counts["tg"])
            p.wait("vector", ttok)
            dst = wbase.rearrange(
                "p (pc n) -> p pc n", n=DI)[:, :, ib * 128:(ib + 1) * 128]
            wq_ev = p.op("vector", lambda v, dst=dst: v.tensor_copy(
                dst, TRQ[:].rearrange("p (c n) -> p c n", n=128)), sig="vg")
    p.wait("vector", wq_ev)
    fvv(lambda v: v.tensor_scalar(sg_tab[:], sg_tab[:], sx_t[:],
                                  1.0 / (127.0 * 127.0), ALU.mult, ALU.mult))
    tok = p.op("vector", lambda v: v.tensor_scalar(
        su_tab[:], su_tab[:], sx_t[:], 1.0 / (127.0 * 127.0),
        ALU.mult, ALU.mult), sig="vg")
    sc_tok = tok
    p.wait("vector", sc_tok)
    p.wait("scalar", sc_tok)

    fr_xn = None
    p.waits("sync", [("fv", p.counts["fv"]), ("vg", p.counts["vg"])])
    fr_xq = [None] * 4
    fr_xqfm = None
    fr_h = [None, None]
    fr_sg = [None, None]
    hacc_par = None
    fr_xbslot = [None, None]
    for tt in range(NT512):
        t0 = tt * 512
        for s4 in range(2):
            sl = s4 % 2
            p.dma("sync", xb[:, sl * D:(sl + 1) * D],
                  xn_dram[t0 + s4 * 128: t0 + (s4 + 1) * 128, :],
                  "ld3x", waits=[t for t in (fr_xbslot[sl],) if t])
        for s4 in range(4):
            sl = s4 % 2
            p.wait("vector", ("ld3x", p.counts["ld3x"] - (16 if s4 < 2
                                                          else 0)))
            if fr_xq[s4]:
                p.wait("vector", fr_xq[s4])
            src_ap = xb[:, sl * D:(sl + 1) * D]
            fvv(lambda v, src_ap=src_ap: v.tensor_scalar(
                src_ap, src_ap, rs_x[:], QCLIP_HI, ALU.mult, ALU.min))
            fvv(lambda v, src_ap=src_ap: v.tensor_scalar(
                src_ap, src_ap, QCLIP_LO, RND, ALU.max, ALU.add))
            tok = p.op("vector", lambda v, s4=s4, src_ap=src_ap:
                       v.tensor_scalar(xq_v(s4), src_ap, RND, None,
                                       ALU.subtract), sig="vg")
            fr_xbslot[sl] = tok
            if s4 < 2:
                p.dma("sync", xb[:, sl * D:(sl + 1) * D],
                      xn_dram[t0 + (s4 + 2) * 128: t0 + (s4 + 3) * 128, :],
                      "ld3x", waits=[tok])
            p.wait("tensor", tok)
            if fr_xqfm:
                p.wait("tensor", fr_xqfm)
            bank = TRQ if s4 % 2 == 0 else TRK
            for c in range(8):
                p.op("tensor", lambda t, c=c, s4=s4, bank=bank: t.transpose(
                    bank[:, c * 128:(c + 1) * 128],
                    xq_v(s4)[:, c * 128:(c + 1) * 128], ident[:]),
                    sig="tg" if c == 7 else None)
            ttok = ("tg", p.counts["tg"])
            p.wait("vector", ttok)
            dst = xnfm_sb[:, 0:4096].rearrange(
                "p (c s f) -> p c s f", s=4, f=128)[:, :, s4, :]
            tok = p.op("vector", lambda v, dst=dst, bank=bank: v.tensor_copy(
                dst, bank[:].rearrange("p (c f) -> p c f", f=128)), sig="vg")
            fr_xq[s4] = tok
        fr_xn = ("vg", p.counts["vg"])
        xq_done = ("vg", p.counts["vg"])
        p.wait("tensor", xq_done)
        for ib in range(NIB):
            q = ib % 2
            gbank = pQ0 if q == 0 else pQ1
            ubank = pK0 if q == 0 else pK1
            if fr_sg[q]:
                p.wait("tensor", fr_sg[q])
            for c in range(8):
                lhsg = wgT_v(c)[:, ib * 128:(ib + 1) * 128]
                lhsu = wuT_v(c)[:, ib * 128:(ib + 1) * 128]
                rhs = xnfm_sb[:, c * 512:(c + 1) * 512]
                p.op("tensor", lambda t, lhsg=lhsg, rhs=rhs, gbank=gbank, c=c:
                     t.matmul(gbank[:], lhsg, rhs,
                              start=(c == 0), stop=(c == 7)))
                p.op("tensor", lambda t, lhsu=lhsu, rhs=rhs, ubank=ubank, c=c:
                     t.matmul(ubank[:], lhsu, rhs,
                              start=(c == 0), stop=(c == 7)),
                     sig="tg" if c == 7 else None)
            gu_tok = ("tg", p.counts["tg"])
            p.wait("scalar", gu_tok)
            stok = p.op("scalar", lambda s, q=q, ib=ib, gbank=gbank:
                        s.activation(sgate_v(q), gbank[:], ACT.Silu,
                                     scale=sg_tab[:, ib:ib + 1]), sig="sg")
            p.wait("vector", stok)
            if fr_h[q]:
                p.wait("vector", fr_h[q])
            htok = p.op("vector", lambda v, q=q, ubank=ubank: v.tensor_tensor(
                hbuf_v(q), sgate_v(q), ubank[:], ALU.mult), sig="vg")
            fr_sg[q] = htok
            p.wait("vector", htok)
            fvv(lambda v, q=q: v.reduce_max(wrw[:], hbuf_v(q), AX,
                                            apply_absolute_value=True))
            np_ = (tt % 2)
            if tt == 0:
                tok = p.op("vector", lambda v, ib=ib, np_=np_: v.tensor_copy(
                    hacc[np_][:, ib:ib + 1], wrw[:]), sig="fv")
            else:
                tok = p.op("vector", lambda v, ib=ib, np_=np_:
                           v.scalar_tensor_tensor(
                               hacc[np_][:, ib:ib + 1], wrw[:], 1.0,
                               hacc[1 - np_][:, ib:ib + 1],
                               ALU.mult, ALU.max), sig="fv")
            p.wait("vector", tok)
            p.wait("gpsimd", htok)
            p.dma("gpsimd", h_dram[ib, :, t0:t0 + 512], hbuf_v(q),
                  "st_h%d" % q)
            fr_h[q] = ("st_h%d" % q, p.counts["st_h%d" % q])
        hacc_par = tt % 2
        fr_xqfm = ("tg", p.counts["tg"])

    fvv(lambda v, hp=hacc_par: v.tensor_tensor(hacc[1 - hp][:], hacc[hp][:],
                                               su_tab[:], ALU.mult))
    tok = p.op("vector", lambda v, hp=hacc_par: v.reduce_max(
        am1[:], hacc[1 - hp][:], AX), sig="vg")
    p.wait("gpsimd", tok)
    t1 = p.dma("gpsimd", amx_dram[:], am1[:], "st_o")
    t2 = p.dma("gpsimd", amrow[:],
               amx_dram[:].rearrange("(o p) c -> o (p c)", o=1), "gld_w2",
               waits=[t1])
    p.wait("vector", t2)
    tok = p.op("vector", lambda v: v.reduce_max(ams, amrow, AX),
               sig="vg")
    p.wait("gpsimd", tok)
    t1 = p.dma("gpsimd", cc2_in[:], ams, "st_o")
    p.wait("gpsimd", t1)
    x2_cc = p.op("gpsimd", lambda g: g.collective_compute(
        "AllReduce", ALU.max, replica_groups=[list(range(8))],
        ins=[cc2_in[:]], outs=[cc2_out[:]]), sig="ccs")

    barrier3 = [("vg", p.counts["vg"]), ("sg", p.counts["sg"]),
                ("tg", p.counts["tg"]),
                ("st_h0", p.counts["st_h0"]), ("st_h1", p.counts["st_h1"])]
    for e in ENGINES:
        p.waits(e, barrier3)

    wdT_v = lambda ip: warena[:, ip * D:(ip + 1) * D]
    HQB = 22 * D
    hq_v = lambda ib, par: warena[:, HQB + par * 11264 + ib * 512:
                                  HQB + par * 11264 + (ib + 1) * 512]
    outst_v = lambda q: xb[:, q * D:(q + 1) * D]

    fr_wd = None
    wd_ev = None
    for mb in range(8):
        tok = p.dma("sync", xb[:, 0:DI], wdown_hbm[mb * 128:(mb + 1) * 128, :],
                    "ld_wd", waits=[fr_wd] if fr_wd else [])
        p.wait("vector", tok)
        fvv(lambda v, mb=mb: v.reduce_max(swd_tab[:, mb:mb + 1], xb[:, 0:DI],
                                          AX, apply_absolute_value=True))
        fvv(lambda v, mb=mb: v.reciprocal(wrw[:], swd_tab[:, mb:mb + 1]))
        fvv(lambda v: v.tensor_scalar(wrw[:], wrw[:], 127.0, None, ALU.mult))
        fvv(lambda v, mb=mb: v.tensor_scalar(
            swd_tab[:, mb:mb + 1], swd_tab[:, mb:mb + 1], 1.0 / 127.0, None,
            ALU.mult))
        fvv(lambda v: v.tensor_scalar(xb[:, 0:DI], xb[:, 0:DI], wrw[:], RND,
                                      ALU.mult, ALU.add))
        tok = p.op("vector", lambda v, mb=mb: v.tensor_scalar(
            xnbf[:, 0:DI], xb[:, 0:DI], RND, swd_tab[:, mb:mb + 1],
            ALU.subtract, ALU.mult), sig="vg")
        fr_wd = tok
        p.wait("tensor", tok)
        if wd_ev:
            p.wait("tensor", wd_ev)
        for ip in range(NIB):
            bank = TRQ if ip % 16 < 8 else TRK
            reg = ip % 8
            p.op("tensor", lambda t, ip=ip, bank=bank, reg=reg: t.transpose(
                bank[:, reg * 128:(reg + 1) * 128],
                xnbf[:, ip * 128:(ip + 1) * 128], ident[:]),
                sig="tg" if (ip % 8 == 7 or ip == NIB - 1) else None)
            if ip % 8 == 7 or ip == NIB - 1:
                ttok = ("tg", p.counts["tg"])
                p.wait("vector", ttok)
                lo = (ip // 8) * 8
                n = ip - lo + 1
                dst = warena[:].rearrange(
                    "p (ip n) -> p ip n", n=D)[:, lo:lo + n,
                                               mb * 128:(mb + 1) * 128]
                wd_ev = p.op("vector", lambda v, dst=dst, bank=bank, n=n:
                             v.tensor_copy(dst, bank[:, 0:n * 128].rearrange(
                                 "p (c f) -> p c f", f=128)), sig="vg")
                p.wait("tensor", wd_ev)
    p.wait("gpsimd", x2_cc)
    t2g = p.dma("gpsimd", ams, cc2_out[:], "gld_w2")
    p.wait("tensor", t2g)
    tok = p.op("tensor", lambda t: t.matmul(
        pK1[:, 0:1], ones1[:], ams, start=True, stop=True), sig="tg")
    p.wait("vector", tok)
    x2_tok = p.op("vector", lambda v: v.tensor_copy(gmax_h[:], pK1[:, 0:1]),
                  sig="vg")
    p.wait("vector", x2_tok)
    fvv(lambda v: v.tensor_tensor(sh_t[:], sgm_d[:], gmax_h[:], ALU.mult))
    fvv(lambda v: v.tensor_scalar(sh_t[:], sh_t[:], 1.0 / 127.0, None,
                                  ALU.mult))
    fvv(lambda v: v.reciprocal(rsh_t[:], sh_t[:]))
    tok = p.op("vector", lambda v: v.tensor_scalar(
        rsq_tab[:], rsq_tab[:], rsh_t[:], None, ALU.mult), sig="vg")
    p.wait("vector", tok)
    dscale_tok = tok

    fr_hl = [fr_wd, fr_wd, fr_wd, fr_wd]
    fr_hqmm = [None, None]
    hq_done = [None, None]
    fr_bank = [None, None]
    fr_os = [None, None]
    fr_xares = [None, None]

    hslot = lambda si: (hbuf_v(si) if si < 2
                        else xb[:, 2048 + si * 512:2048 + (si + 1) * 512])

    def emit_quantD(tt):
        par = tt % 2
        t0q = tt * 512
        if fr_hqmm[par]:
            p.wait("vector", fr_hqmm[par])
        for ib0 in range(0, NIB, 2):
            ibs = [ib0] + ([ib0 + 1] if ib0 + 1 < NIB else [])
            toks = []
            for ib in ibs:
                si = ib % 4
                toks.append(p.dma("sync", hslot(si),
                                  h_dram[ib, :, t0q:t0q + 512],
                                  "ld_h%d" % si,
                                  waits=[t for t in (fr_hl[si],) if t]))
            for tk in toks:
                p.wait("vector", tk)
            for ib in ibs:
                si = ib % 4
                tk = p.op("vector", lambda v, si=si, ib=ib: v.tensor_scalar(
                    hslot(si), hslot(si), rsq_tab[:, ib:ib + 1], QCLIP_HI,
                    ALU.mult, ALU.min), sig="fv" if ib == ibs[-1] else None)
            p.wait("vector", tk)
            for ib in ibs:
                si = ib % 4
                tk = p.op("vector", lambda v, si=si: v.tensor_scalar(
                    hslot(si), hslot(si), QCLIP_LO, RND, ALU.max, ALU.add),
                    sig="fv" if ib == ibs[-1] else None)
            p.wait("vector", tk)
            for ib in ibs:
                si = ib % 4
                tk = p.op("vector", lambda v, si=si, ib=ib, par=par:
                          v.tensor_scalar(hq_v(ib, par), hslot(si), RND,
                                          None, ALU.subtract),
                          sig="vg" if ib == ibs[-1] else None)
            fr_hl[ibs[0] % 4] = fr_hl[ibs[-1] % 4] = tk
        hq_done[par] = ("vg", p.counts["vg"])

    emit_quantD(0)
    for tt in range(NT512):
        t0 = tt * 512
        if tt + 1 < NT512:
            emit_quantD(tt + 1)
        par = tt % 2
        p.wait("tensor", hq_done[par])
        for t4 in range(4):
            b2 = t4 % 2
            bA, bB = (pQ0, pQ1) if b2 == 0 else (pK0, pK1)
            if tt == 0 and t4 == 1:
                p.wait("tensor", x2_tok)
            q4l = t4 % 2
            xw = [fr_xares[q4l]] if fr_xares[q4l] else []
            if tt == 0 and t4 == 0:
                xw.append(dscale_tok)
            xatok = p.dma("scalar", xn_st[:, q4l * D:(q4l + 1) * D],
                          xa_dram[t0 + t4 * 128:t0 + (t4 + 1) * 128, :],
                          "ld_xa", waits=xw)
            if fr_bank[b2]:
                p.wait("tensor", fr_bank[b2])
            for ib in range(NIB):
                lhs = hq_v(ib, par)[:, t4 * 128:(t4 + 1) * 128]
                p.op("tensor", lambda t, lhs=lhs: t.ldweights(lhs))
                p.op("tensor", lambda t, ib=ib, lhs=lhs, bA=bA: t.matmul(
                    bA[:], lhs, wdT_v(ib)[:, 0:512],
                    start=(ib == 0), stop=(ib == NIB - 1)))
                p.op("tensor", lambda t, ib=ib, lhs=lhs, bB=bB: t.matmul(
                    bB[:], lhs, wdT_v(ib)[:, 512:1024],
                    start=(ib == 0), stop=(ib == NIB - 1)),
                    sig="tg" if ib == NIB - 1 else None)
            dtok = ("tg", p.counts["tg"])
            p.wait("vector", dtok)
            p.wait("vector", xatok)
            q4 = t4 % 2
            if fr_os[q4]:
                p.wait("vector", fr_os[q4])
            xa_ap = xn_st[:, q4 * D:(q4 + 1) * D]
            oap = outst_v(q4)
            fvv(lambda v, oap=oap, bA=bA: v.tensor_scalar(
                oap[:, 0:512], bA[:], sh_t[:], None, ALU.mult))
            fvv(lambda v, oap=oap, bB=bB: v.tensor_scalar(
                oap[:, 512:1024], bB[:], sh_t[:], None, ALU.mult))
            tok = p.op("vector", lambda v, xa_ap=xa_ap, oap=oap:
                       v.tensor_tensor(oap, oap, xa_ap, ALU.add), sig="vg")
            fr_bank[b2] = tok
            fr_xares[q4] = tok
            p.wait("gpsimd", tok)
            p.dma("gpsimd", out_hbm[t0 + t4 * 128:t0 + (t4 + 1) * 128, :],
                  oap, "st_o")
            fr_os[q4] = ("st_o", p.counts["st_o"])
        fr_hqmm[par] = ("tg", p.counts["tg"])

    for s in ("st_o", "st_xn0", "st_xn1", "st_qk", "st_fm", "st_xa0",
              "st_xa1", "st_h0", "st_h1"):
        p.wait("gpsimd", (s, p.counts[s]))

    def finalize():
        with nc.Block() as block:
            p.finalize(block)
    return nc, finalize, dict(nb=nb, nch=nch)


def kernel(**inputs):
    NB_TOT, T = 32, 2048
    nb, nch = 4, 16
    x = np.ascontiguousarray(np.asarray(inputs["L0_attn_x"], np.float32))
    x = x.reshape(NB_TOT * T, D)
    nc, finalize, _ = build_program(nb, nch)
    finalize()
    base = {
        "wq": np.ascontiguousarray(inputs["L0_attn_Wq"], np.float32),
        "wk": np.ascontiguousarray(inputs["L0_attn_Wk"], np.float32),
        "wv": np.ascontiguousarray(inputs["L0_attn_Wv"], np.float32),
        "wg": np.ascontiguousarray(inputs["L0_attn_Wg"], np.float32),
        "wo": np.ascontiguousarray(inputs["L0_attn_Wo"], np.float32),
        "gnw": np.ascontiguousarray(inputs["L0_attn_g_norm_w"],
                                    np.float32).reshape(1, DV),
        "clipx": np.ascontiguousarray(inputs["L0_ffn_clip_x"],
                                      np.float32).reshape(1, 1),
        "clipd": np.ascontiguousarray(inputs["L0_ffn_clip_down"],
                                      np.float32).reshape(1, 1),
        "wgate": np.ascontiguousarray(inputs["L0_ffn_w_gate"], np.float32),
        "wup": np.ascontiguousarray(inputs["L0_ffn_w_up"], np.float32),
        "wdown": np.ascontiguousarray(inputs["L0_ffn_w_down"], np.float32),
    }
    ntok = nb * T
    in_maps = [dict(base, x=x[c * ntok:(c + 1) * ntok]) for c in range(8)]
    res = run_bass_kernel_spmd(nc, in_maps, core_ids=list(range(8)))
    out = np.concatenate([r["out"] for r in res.results], 0)
    return out.reshape(NB_TOT, T, D)


# version 17
# speedup vs baseline: 1.0160x; 1.0160x over previous
import os

import numpy as np
import ml_dtypes

import concourse.bass as bass
import concourse.mybir as mybir
from concourse.bass_isa import ReduceOp
from concourse.bass_utils import run_bass_kernel_spmd

F32 = mybir.dt.float32
BF16 = mybir.dt.bfloat16
I32 = mybir.dt.int32
AX = mybir.AxisListType.X
ALU = mybir.AluOpType
ACT = mybir.ActivationFunctionType

MAGIC = 0x5F3759DF
RND = 12582912.0
QCLIP_LO = -128.4999
QCLIP_HI = 127.4999

H, DK, DV, D, DI, CHK = 16, 64, 128, 1024, 2816, 128
NIB = DI // 128


def host_tables(nch):
    t = {}
    t["ident"] = np.eye(128, dtype=ml_dtypes.bfloat16)
    gamma = 1.0 - np.exp2(-5.0 - np.arange(H, dtype=np.float64))

    inv = 1.0 / 10000.0 ** (np.arange(0, DK, 2, dtype=np.float64) / DK)
    pos = (np.arange(nch)[:, None, None] * 128
           + np.arange(128)[None, :, None]).astype(np.float64)
    f = pos * inv[None, None, :]
    cos = np.cos(f).transpose(1, 0, 2).reshape(128, nch * 32)
    sin = np.sin(f).transpose(1, 0, 2).reshape(128, nch * 32)
    c3 = cos.reshape(128, nch, 32)
    s3 = sin.reshape(128, nch, 32)
    t["ct2"] = np.concatenate([c3, c3], -1).reshape(
        128, nch * 64).astype(ml_dtypes.bfloat16)
    t["st2"] = np.concatenate([-s3, s3], -1).reshape(
        128, nch * 64).astype(ml_dtypes.bfloat16)

    i = np.arange(CHK, dtype=np.float64)
    dq64 = np.zeros((64, H * 128), np.float64)
    for h in range(H):
        dq64[:, h * 128:(h + 1) * 128] = gamma[h] ** (i + 1.0)
    t["dqfm"] = (dq64 / 8.0).astype(ml_dtypes.bfloat16)

    j = np.arange(CHK, dtype=np.float64)[:, None]
    dkv = np.repeat(gamma[None, :], 128, 0) ** (CHK - 1.0 - j)
    t["dkv"] = dkv.astype(ml_dtypes.bfloat16)

    rel = i[None, :] - j
    dt = np.zeros((128, H * 128), np.float64)
    for h in range(H):
        dt[:, h * 128:(h + 1) * 128] = np.where(
            rel >= 0, gamma[h] ** np.maximum(rel, 0.0), 0.0)
    t["dt"] = (dt / 8.0).astype(ml_dtypes.bfloat16)

    t["gC"] = (gamma ** CHK).astype(np.float64)
    t["ones_col"] = np.ones((1, 128), np.float32)
    return t


ENGINES = ("sync", "gpsimd", "vector", "scalar", "tensor")


class Plan:

    def __init__(self, nc):
        self.nc = nc
        self.streams = {e: [] for e in ENGINES}
        self.sems = {}
        self.counts = {}
        self.lastwait = {e: {} for e in ENGINES}
        self.guards = []

    def sem(self, name):
        if name not in self.sems:
            g = self.nc.semaphore(name)
            self.sems[name] = g.__enter__()
            self.guards.append(g)
            self.counts[name] = 0
        return self.sems[name]

    def op(self, eng, build, sig=None, sign=1):
        tok = None
        s = None
        if sig:
            self.sem(sig)
            self.counts[sig] += sign
            tok = (sig, self.counts[sig])
            s = self.sems[sig]

        def fn(e, build=build, s=s, sign=sign):
            ins = build(e)
            if s is not None:
                ins.then_inc(s, sign)
        self.streams[eng].append(fn)
        return tok

    def wait(self, eng, tok):
        if tok is None:
            return
        sig, val = tok
        lw = self.lastwait[eng]
        if lw.get(sig, -1) >= val:
            return
        lw[sig] = val
        s = self.sems[sig]
        self.streams[eng].append(lambda e, s=s, val=val: e.wait_ge(s, val))

    def waits(self, eng, toks):
        for t in toks:
            self.wait(eng, t)

    def dma(self, queue, out, in_, sig, waits=()):
        self.waits(queue, waits)
        self.sem(sig)
        if self.counts[sig] > 0:
            self.wait(queue, (sig, self.counts[sig]))
        return self.op(queue, lambda e: e.dma_start(out=out, in_=in_),
                       sig=sig, sign=16)

    def finalize(self, block):
        for e in ENGINES:
            stream = self.streams[e]
            getattr(block, e)(lambda eng, stream=stream: [f(eng) for f in stream])


def emit_rsqrt(p, i32a, i32b, v_ap, out_ap, y_ap, t_ap):
    def fv(build):
        tok = p.op("vector", build, sig="fv")
        p.wait("vector", tok)
    fv(lambda v: v.tensor_scalar(i32a, v_ap.bitcast(I32), 1, None,
                                 ALU.arith_shift_right))
    fv(lambda v: v.tensor_scalar(i32b, i32a, -1.0, float(MAGIC),
                                 ALU.mult, ALU.add))
    y = i32b.bitcast(F32)
    for it in range(3):
        fv(lambda v, y=y: v.tensor_tensor(t_ap, y, y, ALU.mult))
        fv(lambda v: v.tensor_tensor(t_ap, t_ap, v_ap, ALU.mult))
        fv(lambda v: v.tensor_scalar(t_ap, t_ap, -0.5, 1.5, ALU.mult, ALU.add))
        dst = out_ap if it == 2 else y_ap
        fv(lambda v, y=y, dst=dst: v.tensor_tensor(dst, y, t_ap, ALU.mult))
        y = y_ap


def _fake_resid(p, q2, tx, fr_x, fr_xa, xa_v, x_tok, xa_dram, row0):
    p.wait("vector", tx)
    if fr_xa[q2]:
        p.wait("vector", fr_xa[q2])
    tok = p.op("vector", lambda v, q2=q2: v.tensor_copy(xa_v(q2), x_tok(q2)),
               sig="vg")
    fr_x[q2] = tok
    p.wait("gpsimd", tok)
    p.dma("gpsimd", xa_dram[row0:row0 + 128, :], xa_v(q2), "st_xa%d" % q2)
    fr_xa[q2] = ("st_xa%d" % q2, p.counts["st_xa%d" % q2])


def build_program(nb, nch, dbg=False):
    T = nch * 128
    NTOK = nb * T
    NT512 = NTOK // 512
    assert T % 512 == 0

    tabs = host_tables(nch)
    nc = bass.Bass("TRN2", target_bir_lowering=False, debug=False)
    kind_s = "ExternalOutput" if dbg else "Internal"

    x_hbm = nc.dram_tensor("x", [NTOK, D], F32, kind="ExternalInput").ap()
    wq_hbm = nc.dram_tensor("wq", [D, D], F32, kind="ExternalInput").ap()
    wk_hbm = nc.dram_tensor("wk", [D, D], F32, kind="ExternalInput").ap()
    wv_hbm = nc.dram_tensor("wv", [D, 2 * D], F32, kind="ExternalInput").ap()
    wg_hbm = nc.dram_tensor("wg", [D, 2 * D], F32, kind="ExternalInput").ap()
    wo_hbm = nc.dram_tensor("wo", [2 * D, D], F32, kind="ExternalInput").ap()
    gnw_hbm = nc.dram_tensor("gnw", [1, DV], F32, kind="ExternalInput").ap()
    clipx_hbm = nc.dram_tensor("clipx", [1, 1], F32, kind="ExternalInput").ap()
    clipd_hbm = nc.dram_tensor("clipd", [1, 1], F32, kind="ExternalInput").ap()
    wgate_hbm = nc.dram_tensor("wgate", [DI, D], F32, kind="ExternalInput").ap()
    wup_hbm = nc.dram_tensor("wup", [DI, D], F32, kind="ExternalInput").ap()
    wdown_hbm = nc.dram_tensor("wdown", [D, DI], F32, kind="ExternalInput").ap()
    out_hbm = nc.dram_tensor("out", [NTOK, D], F32, kind="ExternalOutput").ap()

    xn_dram = nc.dram_tensor("xn_s", [NTOK, D], F32, kind=kind_s).ap()
    xnfm_dram = nc.dram_tensor("xnfm_s", [8, 128, NTOK], BF16, kind=kind_s).ap()
    qr_dram = nc.dram_tensor("qr_s", [NTOK, D], BF16, kind=kind_s).ap()
    kr_dram = nc.dram_tensor("kr_s", [NTOK, D], BF16, kind=kind_s).ap()
    xa_dram = nc.dram_tensor("xa_s", [NTOK, D], F32, kind=kind_s).ap()
    h_dram = nc.dram_tensor("h_s", [NIB, 128, NTOK], F32, kind=kind_s).ap()
    cc1_in = nc.dram_tensor("cc1_in", [1, 1], F32, kind="Internal").ap()
    cc1_out = nc.dram_tensor("cc1_out", [1, 1], F32, kind="Internal").ap()
    cc2_in = nc.dram_tensor("cc2_in", [1, 1], F32, kind="Internal").ap()
    cc2_out = nc.dram_tensor("cc2_out", [1, 1], F32, kind="Internal").ap()
    amx_dram = nc.dram_tensor("amx_s", [128, 1], F32, kind="Internal").ap()
    swd_dram = nc.dram_tensor("swd_s", [128, 8], F32, kind="Internal").ap()
    if dbg:
        dbg_qfm = nc.dram_tensor("dbg_qfm", [64, 2048], BF16,
                                 kind="ExternalOutput").ap()
        dbg_kfm = nc.dram_tensor("dbg_kfm", [64, 2048], BF16,
                                 kind="ExternalOutput").ap()
        dbg_qdq = nc.dram_tensor("dbg_qdq", [64, 2048], BF16,
                                 kind="ExternalOutput").ap()
        dbg_adt = nc.dram_tensor("dbg_adt", [128, 2048], BF16,
                                 kind="ExternalOutput").ap()
        dbg_osb = nc.dram_tensor("dbg_osb", [128, 2048], BF16,
                                 kind="ExternalOutput").ap()
        dbg_z = nc.dram_tensor("dbg_z", [128, 2048], BF16,
                               kind="ExternalOutput").ap()
        dbg_sf = nc.dram_tensor("dbg_sf", [64, 2048], F32,
                                kind="ExternalOutput").ap()
        dbg_f16 = nc.dram_tensor("dbg_f16", [128, 16], F32,
                                 kind="ExternalOutput").ap()
        dbg_pq = nc.dram_tensor("dbg_pq", [64, 512], F32,
                                kind="ExternalOutput").ap()
        dbg_kdkv = nc.dram_tensor("dbg_kdkv", [128, 1024], BF16,
                                  kind="ExternalOutput").ap()

    ident_h = nc.inline_tensor(tabs["ident"], "ident_t").ap()
    ct2_h = nc.inline_tensor(tabs["ct2"], "ct2_t").ap()
    st2_h = nc.inline_tensor(tabs["st2"], "st2_t").ap()

    guards = []

    def sb(name, shape, dtype):
        g = nc.sbuf_tensor(name, shape, dtype)
        h = g.__enter__()
        guards.append(g)
        return h

    def psum(name, shape, dtype=F32):
        g = nc.psum_tensor(name, shape, dtype)
        h = g.__enter__()
        guards.append(g)
        return h

    ident = sb("identB", [128, 128], BF16)
    ct2_sb = sb("ct2s", [128, nch * 64], BF16)
    st2_sb = sb("st2s", [128, nch * 64], BF16)

    dqfm_h = nc.inline_tensor(tabs["dqfm"], "dqfm_t").ap()
    dkv_h = nc.inline_tensor(tabs["dkv"], "dkv_t").ap()
    dt_h = nc.inline_tensor(tabs["dt"], "dt_t").ap()
    ones_h = nc.inline_tensor(tabs["ones_col"], "ones_t").ap()
    gC = tabs["gC"]
    dqfm = sb("dqfm", [64, 2048], BF16)
    dkv = sb("dkv", [128, 16], BF16)
    dtt = sb("dtt", [128, 2048], BF16)
    ones1 = sb("ones1", [1, 128], F32)
    gnw_sb = sb("gnw_sb", [1, 128], F32)
    gnw_col = sb("gnw_col", [128, 1], F32)
    warena = sb("warena", [128, 49152], BF16)
    wq_sb = warena[:, 0:8 * D]
    wk_sb = warena[:, 8 * D:16 * D]
    xb = sb("xb", [128, 4 * D], F32)
    xn_st = sb("xn_st", [128, 2 * D], F32)
    xnbf = sb("xnbf", [128, 4 * D], BF16)
    xnfm_sb = sb("xnfm_sb", [128, 4096], BF16)
    qr_sb = sb("qr_sb", [128, 4 * D], BF16)
    kr_sb = sb("kr_sb", [128, 4 * D], BF16)
    rope4 = sb("rope4", [128, 1024], BF16)
    sums = sb("sums", [128, 4], F32)
    sqs = sb("sqs", [128, 4], F32)
    sqdump = sb("sqdump", [128, 128], BF16)
    st_m = sb("st_m", [128, 4], F32)
    st_v = sb("st_v", [128, 4], F32)
    st_y = sb("st_y", [128, 4], F32)
    st_t = sb("st_t", [128, 4], F32)
    st_i1 = sb("st_i1", [128, 4], I32)
    st_i2 = sb("st_i2", [128, 4], I32)
    ln_r = sb("ln_r", [128, 4], F32)
    ln_mr = sb("ln_mr", [128, 4], F32)
    am_cols = sb("am_cols", [128, max(NTOK // 128, 1)], F32)
    am1 = sb("am1", [128, 1], F32)
    am1g = sb("am1g", [128, 1], F32)
    amrow = None
    gmax_x = sb("gmax_x", [128, 1], F32)
    gmax_h = sb("gmax_h", [128, 1], F32)
    clipx_sb = sb("clipx_sb", [1, 1], F32)
    clipd_sb = sb("clipd_sb", [1, 1], F32)
    sgm_x = sb("sgm_x", [128, 1], F32)
    sgm_d = sb("sgm_d", [128, 1], F32)
    sx_t = sb("sx_t", [128, 1], F32)
    rs_x = sb("rs_x", [128, 1], F32)
    sh_t = sb("sh_t", [128, 1], F32)
    wrw = sb("wrw", [128, 1], F32)
    rsh_t = wrw
    swg_tab = sb("swg_tab", [128, NIB], F32)
    swu_tab = sb("swu_tab", [128, NIB], F32)
    sg_tab = swg_tab
    su_tab = swu_tab
    rsq_tab = su_tab
    swd_tab = sb("swd_tab", [128, 8], F32)
    hacc = [sb("hacc0", [128, NIB], F32), sb("hacc1", [128, NIB], F32)]
    hm1 = None


    ldbuf = xnfm_sb
    zbuf = sb("zbuf", [128, 4096], BF16)
    S_f = [sb("S_f0", [64, 2048], F32), sb("S_f1", [64, 2048], F32)]
    S_b = [sb("S_b0", [64, 2048], BF16), sb("S_b1", [64, 2048], BF16)]
    qdq64 = sb("qdq64", [64, 2048], BF16)
    ss16 = sb("ss16", [128, 16], F32)
    f16 = sb("f16", [128, 16], F32)
    rs_i1 = sb("rs_i1", [128, 16], I32)
    rs_i2 = sb("rs_i2", [128, 16], I32)
    rs_y = sb("rs_y", [128, 16], F32)
    rs_t = sb("rs_t", [128, 16], F32)

    pT0 = psum("pT0", [128, 1024], BF16)
    pT1 = psum("pT1", [128, 1024], BF16)
    pQ0 = psum("pQ0", [128, 512])
    pQ1 = psum("pQ1", [128, 512])
    pK0 = psum("pK0", [128, 512])
    pK1 = psum("pK1", [128, 512])
    pR0 = psum("pR0", [128, 512])
    pR1 = psum("pR1", [128, 512])
    TRQ, TRK, TRZa, TRZb = pT0, pT1, pT0, pT1

    p = Plan(nc)
    p.sem("gg")
    for s in ("ld_tab", "gld_w", "ld_x", "st_xn", "st_qk", "st_fm",
              "ld_o", "st_o", "fv", "fs", "tg", "vg", "sg", "gld_w2",
              "ld2x_0", "ld2x_1", "ld2fm_0", "ld2fm_1",
              "ld2q_0", "ld2k_0",
              "st_xa0", "st_xa1", "st_xn0", "st_xn1"):
        p.sem(s)

    for ap_sb, ap_h in ((ident, ident_h),
                        (ct2_sb, ct2_h), (st2_sb, st2_h), (dqfm, dqfm_h),
                        (dkv, dkv_h), (dtt, dt_h), (ones1, ones_h),
                        (gnw_sb, gnw_hbm), (clipx_sb, clipx_hbm),
                        (clipd_sb, clipd_hbm)):
        p.dma("sync", ap_sb[:], ap_h[:], "ld_tab")
    tok_tab = ("ld_tab", p.counts["ld_tab"])
    for c in range(16):
        p.dma("gpsimd", warena[:, 16 * D + c * D:16 * D + (c + 1) * D],
              wo_hbm[c * 128:(c + 1) * 128, :], "gld_w2")
    tok_wo_raw = ("gld_w2", p.counts["gld_w2"])
    for c in range(8):
        p.dma("gpsimd", wq_sb[:, c * D:(c + 1) * D],
              wq_hbm[c * 128:(c + 1) * 128, :], "gld_w")
        p.dma("gpsimd", wk_sb[:, c * D:(c + 1) * D],
              wk_hbm[c * 128:(c + 1) * 128, :], "gld_w")
    tok_wqk = ("gld_w", p.counts["gld_w"])

    p.wait("vector", tok_tab)
    p.wait("tensor", tok_tab)
    p.wait("tensor", tok_wqk)

    def rope_half(bank, dst512, cblk, last_sig=None):
        ct = ct2_sb[:, cblk * 64:(cblk + 1) * 64].rearrange(
            "p (o j) -> p o j", o=1).broadcast_to([128, 8, 64])
        sl = st2_sb[:, cblk * 64:cblk * 64 + 32].rearrange(
            "p (o j) -> p o j", o=1).broadcast_to([128, 8, 32])
        sh = st2_sb[:, cblk * 64 + 32:(cblk + 1) * 64].rearrange(
            "p (o j) -> p o j", o=1).broadcast_to([128, 8, 32])
        sv = bank[:].rearrange("p (h sj) -> p h sj", sj=64)
        x1, x2 = sv[:, :, 0:32], sv[:, :, 32:64]
        rA_f = rope4[:, 0:512]
        rB_f = rope4[:, 512:1024]
        rAh = rA_f.rearrange("p (h j) -> p h j", j=64)
        rBh = rB_f.rearrange("p (h j) -> p h j", j=64)
        p.op("vector", lambda v: v.tensor_tensor(rAh, sv, ct, ALU.mult))
        p.op("vector", lambda v: v.tensor_tensor(
            rBh[:, :, 0:32], x2, sl, ALU.mult))
        tok = p.op("vector", lambda v: v.tensor_tensor(
            rBh[:, :, 32:64], x1, sh, ALU.mult), sig="fv")
        p.wait("vector", tok)
        tok = p.op("vector", lambda v: v.tensor_tensor(
            dst512, rA_f, rB_f, ALU.add), sig=last_sig or "fv")
        if last_sig is None:
            p.wait("vector", tok)
        return tok

    trev_tok = [None, None]
    xnrot_cast = [None, None]
    xnrot_st = [None, None]
    b1s = {"ln_last": None,
           "norm_last": None,
           "stqk": None, "stfm": None,
           "lntok": {},
           "casts": {}}

    def b1_A1(t):
        t0 = t * 512
        lw = [b1s["norm_last"]] if b1s["norm_last"] else []
        for s4 in range(4):
            p.dma("sync", xb[:, s4 * D:(s4 + 1) * D],
                  x_hbm[t0 + s4 * 128: t0 + (s4 + 1) * 128, :],
                  "ld_x", waits=lw)
            lw = []
        xall = ("ld_x", p.counts["ld_x"])
        if b1s["ln_last"]:
            p.wait("vector", b1s["ln_last"])
            p.wait("scalar", b1s["ln_last"])
        p.wait("vector", xall)
        p.wait("scalar", xall)
        for s4 in range(4):
            p.op("vector", lambda v, s4=s4: v.reduce_sum(
                sums[:, s4:s4 + 1], xb[:, s4 * D:(s4 + 1) * D], AX),
                sig="vg" if s4 == 3 else None)
            p.op("scalar", lambda s, s4=s4: s.activation(
                zbuf[:, s4 * D:(s4 + 1) * D], xb[:, s4 * D:(s4 + 1) * D],
                ACT.Square, accum_out=sqs[:, s4:s4 + 1]),
                sig="sg" if s4 == 3 else None)
        sq_tok = ("sg", p.counts["sg"])

        def fv(build):
            tk = p.op("vector", build, sig="fv")
            p.wait("vector", tk)
        p.wait("vector", sq_tok)
        p.wait("vector", ("vg", p.counts["vg"]))
        if b1s["norm_last"]:
            p.wait("vector", b1s["norm_last"])
        fv(lambda v: v.tensor_scalar(st_m[:], sums[:], 1.0 / D, None,
                                     ALU.mult))
        fv(lambda v: v.tensor_tensor(st_v[:], st_m[:], st_m[:], ALU.mult))
        fv(lambda v: v.scalar_tensor_tensor(
            st_v[:], sqs[:], 1.0 / D, st_v[:], ALU.mult, ALU.subtract))
        fv(lambda v: v.tensor_scalar(st_v[:], st_v[:], 1e-5, None, ALU.add))
        emit_rsqrt(p, st_i1[:], st_i2[:], st_v[:], ln_r[:], st_y[:], st_t[:])
        fv(lambda v: v.tensor_tensor(ln_mr[:], st_m[:], ln_r[:], ALU.mult))
        tok = p.op("vector", lambda v: v.tensor_scalar(
            ln_mr[:], ln_mr[:], -1.0, None, ALU.mult), sig="vg")
        b1s["ln_last"] = tok
        b1s["lntok"][t] = tok

    def b1_A2_s4(t, s4, extra_war=None):
        g = t * 4 + s4
        rot = s4 % 2
        xn_ap = xn_st[:, rot * D:(rot + 1) * D]
        wts = [tk for tk in (xnrot_cast[rot], xnrot_st[rot]) if tk]
        p.waits("scalar", wts)
        p.wait("scalar", b1s["lntok"][t])
        tok = p.op("scalar", lambda s, s4=s4, t=t, xn_ap=xn_ap: s.activation(
            xn_ap, xb[:, s4 * D:(s4 + 1) * D], ACT.Identity,
            bias=ln_mr[:, s4:s4 + 1], scale=ln_r[:, s4:s4 + 1]), sig="sg")
        b1s["norm_last"] = tok
        p.wait("vector", tok)
        p.wait("gpsimd", tok)
        p.op("vector", lambda v, g=g, xn_ap=xn_ap: v.reduce_max(
            am_cols[:, g:g + 1], xn_ap, AX, apply_absolute_value=True))
        p.dma("gpsimd", xn_dram[g * 128:(g + 1) * 128, :], xn_ap,
              "st_xn%d" % rot)
        xnrot_st[rot] = ("st_xn%d" % rot, p.counts["st_xn%d" % rot])
        if extra_war:
            p.wait("vector", extra_war)
        tok = p.op("vector", lambda v, s4=s4, xn_ap=xn_ap: v.tensor_copy(
            xnbf[:, s4 * D:(s4 + 1) * D], xn_ap), sig="vg")
        xnrot_cast[rot] = tok
        b1s["casts"].setdefault(t, {})[s4] = tok

    def b1_B(t):
        t0 = t * 512
        g0 = t * 4
        p.waits("vector", [tk for tk in (b1s["stqk"], b1s["stfm"]) if tk])
        if b1s["stfm"]:
            p.wait("scalar", b1s["stfm"])
        for s4 in range(4):
            g = g0 + s4
            cblk = g % nch
            xnb_ap = xnbf[:, s4 * D:(s4 + 1) * D]
            bank = pT0 if s4 % 2 == 0 else pT1
            if trev_tok[s4 % 2] is not None:
                p.wait("tensor", trev_tok[s4 % 2])
            p.wait("tensor", b1s["casts"][t][s4])
            for c in range(8):
                p.op("tensor", lambda t_, c=c, bank=bank, xnb_ap=xnb_ap:
                     t_.transpose(bank[:, c * 128:(c + 1) * 128],
                                  xnb_ap[:, c * 128:(c + 1) * 128],
                                  ident[:]), sig="tg" if c == 7 else None)
            ttok = ("tg", p.counts["tg"])
            p.wait("scalar", ttok)
            dst = xnfm_sb[:, 0:4096].rearrange(
                "p (c s f) -> p c s f", s=4, f=128)[:, :, s4, :]
            etok = p.op("scalar", lambda s, bank=bank, dst=dst: s.activation(
                dst, bank[:].rearrange("p (c f) -> p c f", f=128), ACT.Copy),
                sig="sg")
            trev_tok[s4 % 2] = etok
            if t + 1 < NT512:
                b1_A2_s4(t + 1, s4, extra_war=ttok)
            p.wait("tensor", etok)
            if b1s.get("rope_prev"):
                p.wait("tensor", b1s["rope_prev"])
            for c in range(8):
                lhs = xnfm_sb[:, c * 512 + s4 * 128: c * 512 + s4 * 128 + 128]
                p.op("tensor", lambda t_, lhs=lhs: t_.ldweights(lhs))
                for bi, (bank2, w_sb, off) in enumerate((
                        (pQ0, wq_sb, 0), (pQ1, wq_sb, 512),
                        (pK0, wk_sb, 0), (pK1, wk_sb, 512))):
                    rhs = w_sb[:, c * D + off: c * D + off + 512]
                    p.op("tensor", lambda t_, bank2=bank2, lhs=lhs, rhs=rhs,
                         c=c: t_.matmul(bank2[:], lhs, rhs,
                                        start=(c == 0), stop=(c == 7)),
                         sig="tg" if (c == 7 and bi == 3) else None)
            mtok = ("tg", p.counts["tg"])
            p.wait("vector", mtok)
            qdst = qr_sb[:, s4 * D:(s4 + 1) * D]
            kdst = kr_sb[:, s4 * D:(s4 + 1) * D]
            rope_half(pQ0, qdst[:, 0:512], cblk)
            rope_half(pQ1, qdst[:, 512:1024], cblk)
            rope_half(pK0, kdst[:, 0:512], cblk)
            rtok = rope_half(pK1, kdst[:, 512:1024], cblk, last_sig="vg")
            b1s["rope_prev"] = rtok
            p.wait("gpsimd", rtok)
            p.dma("gpsimd", qr_dram[g * 128:(g + 1) * 128, :], qdst, "st_qk")
            p.dma("gpsimd", kr_dram[g * 128:(g + 1) * 128, :], kdst, "st_qk")
        p.wait("gpsimd", ("vg", p.counts["vg"]))
        for c in range(8):
            p.dma("gpsimd", xnfm_dram[c, :, t0:t0 + 512],
                  xnfm_sb[:, c * 512:(c + 1) * 512], "st_fm")
        b1s["stqk"] = ("st_qk", p.counts["st_qk"])
        b1s["stfm"] = ("st_fm", p.counts["st_fm"])

    b1_A1(0)
    for s4 in range(4):
        b1_A2_s4(0, s4)
    for tt in range(NT512):
        if tt + 1 < NT512:
            b1_A1(tt + 1)
        b1_B(tt)

    amrow = sqdump[0:1, 0:256].bitcast(F32) if False else xn_st[0:1, 0:128]
    ams = xn_st[0:1, 128:129]
    for s in ("ccs",):
        p.sem(s)
    p.wait("vector", ("sg", p.counts["sg"]))
    tok = p.op("vector", lambda v: v.reduce_max(
        am1[:], am_cols[:], AX, apply_absolute_value=False), sig="vg")
    p.wait("gpsimd", tok)
    t1 = p.dma("gpsimd", amx_dram[:], am1[:], "st_o")
    t2 = p.dma("gpsimd", amrow,
               amx_dram[:].rearrange("(o p) c -> o (p c)", o=1), "gld_w2",
               waits=[t1, ("st_xn0", p.counts["st_xn0"]),
                      ("st_xn1", p.counts["st_xn1"])])
    p.wait("vector", t2)
    tok = p.op("vector", lambda v: v.reduce_max(ams, amrow, AX),
               sig="vg")
    p.wait("gpsimd", tok)
    t1 = p.dma("gpsimd", cc1_in[:], ams, "st_o")
    p.wait("gpsimd", t1)
    x1_cc = p.op("gpsimd", lambda g: g.collective_compute(
        "AllReduce", ALU.max, replica_groups=[list(range(8))],
        ins=[cc1_in[:]], outs=[cc1_out[:]]), sig="ccs")

    NCHG = nb * nch
    barrier = [("tg", p.counts["tg"]), ("vg", p.counts["vg"]),
               ("sg", p.counts["sg"]),
               ("st_fm", p.counts["st_fm"]), ("st_qk", p.counts["st_qk"]),
               ("st_xn0", p.counts["st_xn0"]), ("st_xn1", p.counts["st_xn1"])]
    for e in ENGINES:
        p.waits(e, barrier)

    wv_v = lambda c: warena[:, c * 2048:(c + 1) * 2048]
    wg_v = lambda c: warena[:, 16 * D + c * 2048: 16 * D + (c + 1) * 2048]
    woraw_v = lambda c: warena[:, 16 * D + c * D: 16 * D + (c + 1) * D]
    wo_v = lambda c: warena[:, 32 * D + c * D: 32 * D + (c + 1) * D]
    x_tok = lambda q: xb[:, q * D:(q + 1) * D]
    xa_v = lambda q: xn_st[:, q * D:(q + 1) * D]
    v_tok = xnbf[:, 0:2048]
    gsil = xnbf[:, 2048:4096]
    xnfm_ch = lambda q: ldbuf[:, q * 1024:(q + 1) * 1024]
    qr_ch = lambda q: ldbuf[:, 2048:3072]
    kr_ch = lambda q: ldbuf[:, 3072:4096]
    qfm = qr_sb[0:64, 0:2048]
    kfm = qr_sb[0:64, 2048:4096]
    kdkv = rope4[:, 0:1024]
    adt = kr_sb[:, 0:2048]
    o_sb = kr_sb[:, 2048:4096]
    z_tok = zbuf[:, 0:2048]
    zfm = zbuf[:, 2048:4096]
    sqd2 = sqdump[:, 0:128]

    B2P = int(os.environ.get("B2_PRO", "1"))
    p.wait("tensor", tok_tab)
    if B2P:
        tok = p.op("tensor", lambda t: t.matmul(
            pR0[:, 0:128], gnw_sb[:], ones1[:], start=True, stop=True),
            sig="tg")
        p.wait("vector", tok)
        tok = p.op("vector", lambda v: v.tensor_copy(gnw_col[:], pR0[:, 0:1]),
                   sig="fv")
        p.wait("vector", tok)
        p.wait("vector", tok_wo_raw)
        for c in range(16):
            p.op("vector", lambda v, c=c: v.tensor_scalar(
                wo_v(c), woraw_v(c), gnw_col[:], None, ALU.mult),
                sig="vg" if c == 15 else None)
    else:
        p.wait("vector", tok_wo_raw)
        p.op("vector", lambda v: v.tensor_copy(wo_v(0), woraw_v(0)),
             sig="vg")
    tok_wo = ("vg", p.counts["vg"])
    p.wait("gpsimd", tok_wo)
    for c in range(8):
        p.dma("gpsimd", wv_v(c), wv_hbm[c * 128:(c + 1) * 128, :], "gld_w2")
        p.dma("gpsimd", wg_v(c), wg_hbm[c * 128:(c + 1) * 128, :], "gld_w2")
    tok_wvg = ("gld_w2", p.counts["gld_w2"])
    p.wait("tensor", tok_wvg)

    fr_x = [None, None]
    fr_fm = [None, None]
    fr_q = [None, None]
    fr_k = [None, None]
    fr_xa = [None, None]
    fr_vg = None
    fr_qkfm = None
    fr_adt = None
    fr_z = None
    fr_zfm = None
    fr_T = None
    fr_osb = None
    fr_scast = [None, None]
    fr_sstt = [None, None]
    fr_ss = None

    for b in range(nb):
        pz = 0
        p.waits("vector", [t for t in (fr_scast[pz], fr_sstt[pz]) if t])
        tok = p.op("vector", lambda v: v.memset(S_f[pz][:], 0.0), sig="fv")
        p.wait("vector", tok)
        tok = p.op("vector", lambda v: v.memset(S_b[pz][:], 0.0), sig="vg")
        fr_scast[pz] = tok
        p.wait("tensor", tok)
        B2S = int(os.environ.get("B2_STAGE", "9"))
        for ci in range(nch):
            kg = b * nch + ci
            q2 = kg % 2
            row0 = (b * T + ci * 128)
            s_in = ci % 2
            s_out = (ci + 1) % 2
            tx = p.dma("sync", x_tok(q2), x_hbm[row0:row0 + 128, :],
                       "ld2x_%d" % q2,
                       waits=[t for t in (fr_x[q2],) if t])
            tfm = p.dma("sync",
                        xnfm_ch(q2).rearrange("p (c t) -> p c t", t=128),
                        xnfm_dram[:, :, row0:row0 + 128].transpose([1, 0, 2]),
                        "ld2fm_%d" % q2,
                        waits=[t for t in (fr_fm[q2],) if t])
            tq = p.dma("sync", qr_ch(q2), qr_dram[row0:row0 + 128, :],
                       "ld2q_0",
                       waits=[t for t in (fr_q[0],) if t])
            tk = p.dma("sync", kr_ch(q2), kr_dram[row0:row0 + 128, :],
                       "ld2k_0",
                       waits=[t for t in (fr_k[0], fr_k[1]) if t])
            if B2S < 1:
                _fake_resid(p, q2, tx, fr_x, fr_xa, xa_v, x_tok, xa_dram, row0)
                fr_fm[q2] = tfm
                fr_q[0] = tq
                fr_k[0] = tk
                continue
            p.wait("tensor", tfm)
            if fr_vg:
                p.wait("tensor", fr_vg)
            w_banks = (pQ0, pQ1, pK0, pK1)
            for c in range(8):
                lhs = xnfm_ch(q2)[:, c * 128:(c + 1) * 128]
                p.op("tensor", lambda t, lhs=lhs: t.ldweights(lhs))
                for bi in range(4):
                    w_ap = wv_v(c)[:, bi * 512:(bi + 1) * 512]
                    p.op("tensor", lambda t, bi=bi, lhs=lhs, w_ap=w_ap, c=c:
                         t.matmul(w_banks[bi][:], lhs, w_ap,
                                  start=(c == 0), stop=(c == 7)),
                         sig="tg" if (c == 7 and bi == 3) else None)
            t_v = ("tg", p.counts["tg"])
            p.wait("vector", t_v)
            ev_v = []
            for bi, bk in enumerate(w_banks):
                ev_v.append(p.op("vector", lambda v, bi=bi, bk=bk:
                                 v.tensor_copy(
                                     v_tok[:, bi * 512:(bi + 1) * 512], bk[:]),
                                 sig="vg"))
            p.waits("tensor", ev_v)
            for c in range(8):
                lhs = xnfm_ch(q2)[:, c * 128:(c + 1) * 128]
                p.op("tensor", lambda t, lhs=lhs: t.ldweights(lhs))
                for bi in range(4):
                    w_ap = wg_v(c)[:, bi * 512:(bi + 1) * 512]
                    p.op("tensor", lambda t, bi=bi, lhs=lhs, w_ap=w_ap, c=c:
                         t.matmul(w_banks[bi][:], lhs, w_ap,
                                  start=(c == 0), stop=(c == 7)),
                         sig="tg" if (c == 7 and bi == 3) else None)
            t_g = ("tg", p.counts["tg"])
            p.wait("scalar", t_g)
            ev_g = []
            for bi, bk in enumerate(w_banks):
                ev_g.append(p.op("scalar", lambda s, bi=bi, bk=bk:
                                 s.activation(
                                     gsil[:, bi * 512:(bi + 1) * 512], bk[:],
                                     ACT.Silu), sig="sg"))
            if B2S < 2:
                p.waits("tensor", ev_g)
                _fake_resid(p, q2, tx, fr_x, fr_xa, xa_v, x_tok, xa_dram, row0)
                fr_vg = fr_x[q2]
                fr_fm[q2] = t_g
                fr_q[0] = tq
                fr_k[0] = tk
                continue
            if fr_qkfm:
                p.wait("tensor", fr_qkfm)
            if fr_T:
                p.wait("tensor", fr_T)
            p.wait("tensor", tq)
            qsrc = qr_ch(q2)
            for h in range(H):
                bank = TRQ if h < 8 else TRK
                p.op("tensor", lambda t, h=h, bank=bank, qsrc=qsrc:
                     t.transpose(bank[0:64, (h % 8) * 128:(h % 8 + 1) * 128],
                                 qsrc[:, h * 64:(h + 1) * 64], ident[:]),
                    sig="tg" if h == H - 1 else None)
            fr_q[0] = ("tg", p.counts["tg"])
            p.wait("vector", fr_q[0])
            p.op("vector", lambda v: v.tensor_copy(qfm[:, 0:1024],
                                                   TRQ[0:64, :]))
            tok = p.op("vector", lambda v: v.tensor_copy(
                qfm[:, 1024:2048], TRK[0:64, :]), sig="vg")
            t_qcp = tok
            p.wait("vector", t_qcp)
            tok = p.op("vector", lambda v: v.tensor_tensor(
                qdq64[:], qfm[:], dqfm[:], ALU.mult), sig="vg")
            t_qfm_ev = tok
            p.wait("tensor", tk)
            p.wait("tensor", t_qcp)
            ksrc = kr_ch(q2)
            for h in range(H):
                bank = TRQ if h < 8 else TRK
                p.op("tensor", lambda t, h=h, bank=bank, ksrc=ksrc:
                     t.transpose(bank[0:64, (h % 8) * 128:(h % 8 + 1) * 128],
                                 ksrc[:, h * 64:(h + 1) * 64], ident[:]),
                    sig="tg" if h == H - 1 else None)
            t_ktr = ("tg", p.counts["tg"])
            p.wait("vector", t_ktr)
            p.op("vector", lambda v: v.tensor_copy(kfm[:, 0:1024],
                                                   TRQ[0:64, :]))
            tok = p.op("vector", lambda v: v.tensor_copy(
                kfm[:, 1024:2048], TRK[0:64, :]), sig="vg")
            t_kfm = tok
            p.wait("vector", tk)
            tok = p.op("vector", lambda v, q2=q2: v.tensor_tensor(
                kdkv.rearrange("p (h dk) -> p h dk", dk=64),
                kr_ch(q2)[:].rearrange("p (h dk) -> p h dk", dk=64),
                dkv[:].rearrange("p (h o) -> p h o", o=1).broadcast_to(
                    [128, 16, 64]), ALU.mult), sig="vg")
            t_qk_ev = tok
            fr_k[0] = t_ktr
            fr_k[1] = tok
            if B2S < 3:
                p.waits("tensor", ev_g)
                _fake_resid(p, q2, tx, fr_x, fr_xa, xa_v, x_tok, xa_dram, row0)
                fr_vg = fr_x[q2]
                fr_fm[q2] = t_g
                fr_qkfm = ("tg", p.counts["tg"])
                continue
            p.wait("tensor", t_qk_ev)
            p.waits("tensor", ev_g)
            if fr_zfm:
                p.wait("scalar", fr_zfm)
            adt_toks = [None] * 4
            sq_toks = []
            ocp_toks = []
            stt_toks = []
            for gi in range(4):
                bkA = pR0 if gi % 2 == 0 else pK1
                bkO = pR1 if gi % 2 == 0 else pK0
                bkS = pQ0 if gi % 2 == 0 else pQ1
                if gi >= 2:
                    p.wait("tensor", adt_toks[gi - 2])
                    if len(ocp_toks) > gi - 2:
                        p.wait("tensor", ocp_toks[gi - 2])
                        p.wait("tensor", sq_toks[gi - 2])
                    if len(stt_toks) > gi - 2:
                        p.wait("tensor", stt_toks[gi - 2])
                for hh in range(4):
                    h = gi * 4 + hh
                    lhs = kfm[:, h * 128:(h + 1) * 128]
                    rhs = qfm[:, h * 128:(h + 1) * 128]
                    p.op("tensor", lambda t, hh=hh, lhs=lhs, rhs=rhs, bkA=bkA:
                         t.matmul(bkA[:, hh * 128:(hh + 1) * 128], lhs, rhs,
                                  start=True, stop=True),
                         sig="tg" if hh == 3 else None)
                t_A = ("tg", p.counts["tg"])
                p.wait("vector", t_A)
                atok = p.op("vector", lambda v, gi=gi, bkA=bkA: v.tensor_tensor(
                    adt[:, gi * 512:(gi + 1) * 512], bkA[:],
                    dtt[:, gi * 512:(gi + 1) * 512], ALU.mult), sig="vg")
                adt_toks[gi] = atok
                if B2S < 4:
                    p.wait("tensor", adt_toks[gi])
                    continue
                p.wait("tensor", atok)
                for hh in range(4):
                    h = gi * 4 + hh
                    lhs1 = adt[:, h * 128:(h + 1) * 128]
                    rhs1 = v_tok[:, h * 128:(h + 1) * 128]
                    lhs2 = qdq64[:, h * 128:(h + 1) * 128]
                    rhs2 = S_b[s_in][:, h * 128:(h + 1) * 128]
                    p.op("tensor", lambda t, hh=hh, lhs1=lhs1, rhs1=rhs1,
                         bkO=bkO: t.matmul(bkO[:, hh * 128:(hh + 1) * 128],
                                           lhs1, rhs1, start=True, stop=False))
                    p.op("tensor", lambda t, hh=hh, lhs2=lhs2, rhs2=rhs2,
                         bkO=bkO: t.matmul(bkO[:, hh * 128:(hh + 1) * 128],
                                           lhs2, rhs2, start=False, stop=True),
                         sig="tg" if hh == 3 else None)
                t_o = ("tg", p.counts["tg"])
                p.wait("scalar", t_o)
                if fr_ss:
                    p.wait("scalar", fr_ss)
                for hh in range(4):
                    h = gi * 4 + hh
                    tok = p.op("scalar", lambda s, hh=hh, h=h, bkO=bkO:
                               s.activation(
                                   zfm[:, h * 128:(h + 1) * 128],
                                   bkO[:, hh * 128:(hh + 1) * 128],
                                   ACT.Square, accum_out=ss16[:, h:h + 1]),
                               sig="sg" if hh == 3 else None)
                sq_toks.append(tok)
                tok = p.op("scalar", lambda s, gi=gi, bkO=bkO: s.activation(
                    o_sb[:, gi * 512:(gi + 1) * 512], bkO[:], ACT.Copy),
                    sig="sg")
                ocp_toks.append(tok)
                if B2S < 5:
                    p.wait("tensor", adt_toks[gi])
                    p.wait("tensor", ocp_toks[gi])
                    continue
                for hh in range(4):
                    h = gi * 4 + hh
                    lhs = kdkv[:, h * 64:(h + 1) * 64]
                    rhs = v_tok[:, h * 128:(h + 1) * 128]
                    p.op("tensor", lambda t, hh=hh, lhs=lhs, rhs=rhs, bkS=bkS:
                         t.matmul(bkS[0:64, hh * 128:(hh + 1) * 128],
                                  lhs, rhs, start=True, stop=True),
                         sig="tg" if hh == 3 else None)
                t_s = ("tg", p.counts["tg"])
                p.wait("vector", t_s)
                if dbg and kg == 0 and gi == 0:
                    ptok = p.op("vector", lambda v: v.tensor_copy(
                        xn_st[0:64, 0:512], pQ0[0:64, :]), sig="vg")
                    p.wait("gpsimd", ptok)
                    p.dma("gpsimd", dbg_pq[:], xn_st[0:64, 0:512], "st_o")
                    p.dma("gpsimd", dbg_kdkv[:], kdkv, "st_o")
                    dbgtok = ("st_o", p.counts["st_o"])
                    p.wait("vector", dbgtok)
                    p.wait("scalar", dbgtok)
                p.waits("vector", [t for t in (fr_scast[s_out],
                                               fr_sstt[s_out]) if t])
                for hh in range(4):
                    h = gi * 4 + hh
                    tok = p.op("vector", lambda v, hh=hh, h=h, bkS=bkS,
                               si=s_in, so=s_out: v.scalar_tensor_tensor(
                                   S_f[so][:, h * 128:(h + 1) * 128],
                                   S_f[si][:, h * 128:(h + 1) * 128],
                                   float(gC[h]),
                                   bkS[0:64, hh * 128:(hh + 1) * 128],
                                   ALU.mult, ALU.add),
                               sig="vg" if hh == 3 else None)
                stt_toks.append(("vg", p.counts["vg"]))
            for tk in (ocp_toks[-2:] + sq_toks[-2:]):
                p.wait("tensor", tk)
            fr_vg = ("tg", p.counts["tg"])
            fr_qkfm = fr_vg
            if B2S < 5:
                if B2S < 4:
                    sq_toks = []
                    ocp_toks = []
                p.waits("tensor", ev_g)
                p.waits("tensor", ocp_toks)
                _fake_resid(p, q2, tx, fr_x, fr_xa, xa_v, x_tok, xa_dram, row0)
                fr_vg = fr_x[q2]
                fr_fm[q2] = t_g
                continue
            p.waits("vector", stt_toks)
            tok = p.op("vector", lambda v, so=s_out: v.tensor_copy(
                S_b[so][:], S_f[so][:]), sig="vg")
            fr_scast[s_out] = tok
            fr_sstt[s_out] = stt_toks[-1]
            p.wait("tensor", tok)
            p.waits("vector", sq_toks)
            if fr_ss:
                p.wait("vector", fr_ss)
            tok = p.op("vector", lambda v: v.tensor_scalar(
                rs_t[:], ss16[:], 1.0 / DV, 1e-5, ALU.mult, ALU.add),
                sig="fv")
            p.wait("vector", tok)
            emit_rsqrt(p, rs_i1[:], rs_i2[:], rs_t[:], f16[:], rs_y[:],
                       ss16[:])
            fr_ss = ("fv", p.counts["fv"])
            if B2S < 6:
                p.waits("tensor", ev_g)
                _fake_resid(p, q2, tx, fr_x, fr_xa, xa_v, x_tok, xa_dram, row0)
                fr_vg = fr_x[q2]
                fr_fm[q2] = t_g
                continue
            p.waits("vector", ocp_toks)
            p.waits("vector", ev_g)
            if fr_z:
                p.wait("vector", fr_z)
            for h in range(H):
                p.op("vector", lambda v, h=h: v.scalar_tensor_tensor(
                    z_tok[:, h * 128:(h + 1) * 128],
                    o_sb[:, h * 128:(h + 1) * 128], f16[:, h:h + 1],
                    gsil[:, h * 128:(h + 1) * 128], ALU.mult, ALU.mult),
                    sig="vg" if h == H - 1 else None)
            t_z = ("vg", p.counts["vg"])
            fr_osb = t_z
            if B2S < 7:
                p.waits("tensor", ev_g)
                _fake_resid(p, q2, tx, fr_x, fr_xa, xa_v, x_tok, xa_dram, row0)
                fr_vg = fr_x[q2]
                fr_fm[q2] = t_g
                fr_z = t_z
                continue
            p.wait("tensor", t_z)
            p.wait("tensor", t_qfm_ev)
            p.wait("tensor", t_kfm)
            if fr_zfm:
                p.wait("tensor", fr_zfm)
            for h in range(8):
                p.op("tensor", lambda t, h=h: t.transpose(
                    TRZa[:, h * 128:(h + 1) * 128],
                    z_tok[:, h * 128:(h + 1) * 128], ident[:]),
                    sig="tg" if h == 7 else None)
            t_za = ("tg", p.counts["tg"])
            for h in range(8):
                p.op("tensor", lambda t, h=h: t.transpose(
                    TRZb[:, h * 128:(h + 1) * 128],
                    z_tok[:, 1024 + h * 128:1024 + (h + 1) * 128], ident[:]),
                    sig="tg" if h == 7 else None)
            t_zb = ("tg", p.counts["tg"])
            fr_z = t_zb
            p.wait("vector", t_za)
            p.op("vector", lambda v: v.tensor_copy(zfm[:, 0:1024], TRZa[:]))
            p.wait("vector", t_zb)
            tok = p.op("vector", lambda v: v.tensor_copy(
                zfm[:, 1024:2048], TRZb[:]), sig="vg")
            t_zfm = tok
            fr_T = tok
            if B2S < 8:
                p.waits("tensor", ev_g)
                _fake_resid(p, q2, tx, fr_x, fr_xa, xa_v, x_tok, xa_dram, row0)
                fr_vg = fr_x[q2]
                fr_fm[q2] = t_g
                fr_zfm = t_zfm
                continue
            p.wait("tensor", t_zfm)
            for h in range(16):
                lhs = zfm[:, h * 128:(h + 1) * 128]
                p.op("tensor", lambda t, lhs=lhs: t.ldweights(lhs))
                p.op("tensor", lambda t, h=h, lhs=lhs: t.matmul(
                    pQ1[:], lhs, wo_v(h)[:, 0:512],
                    start=(h == 0), stop=(h == 15)))
                p.op("tensor", lambda t, h=h, lhs=lhs: t.matmul(
                    pK0[:], lhs, wo_v(h)[:, 512:1024],
                    start=(h == 0), stop=(h == 15)),
                    sig="tg" if h == 15 else None)
            t_wo = ("tg", p.counts["tg"])
            fr_zfm = t_wo
            p.wait("vector", t_wo)
            p.wait("vector", tx)
            if fr_xa[q2]:
                p.wait("vector", fr_xa[q2])
            p.op("vector", lambda v, q2=q2: v.tensor_tensor(
                xa_v(q2)[:, 0:512], pQ1[:], x_tok(q2)[:, 0:512], ALU.add))
            tok = p.op("vector", lambda v, q2=q2: v.tensor_tensor(
                xa_v(q2)[:, 512:1024], pK0[:], x_tok(q2)[:, 512:1024],
                ALU.add), sig="vg")
            fr_x[q2] = tok
            p.wait("gpsimd", tok)
            p.dma("gpsimd", xa_dram[row0:row0 + 128, :], xa_v(q2),
                  "st_xa%d" % q2)
            fr_xa[q2] = ("st_xa%d" % q2, p.counts["st_xa%d" % q2])
            fr_fm[q2] = t_g
            fr_vg = fr_x[q2]
            if dbg and kg == 0:
                for dst, apx in ((dbg_qfm, qfm), (dbg_kfm, kfm),
                                 (dbg_qdq, qdq64[:]), (dbg_adt, adt),
                                 (dbg_osb, o_sb), (dbg_z, z_tok),
                                 (dbg_sf, S_f[s_out][:]),
                                 (dbg_f16, f16[:])):
                    p.dma("gpsimd", dst[:] if dst.shape else dst, apx, "st_o")
                for e in ENGINES:
                    p.wait(e, ("st_o", p.counts["st_o"]))

    barrier2 = [("vg", p.counts["vg"]), ("sg", p.counts["sg"]),
                ("tg", p.counts["tg"]),
                ("st_xa0", p.counts["st_xa0"]),
                ("st_xa1", p.counts["st_xa1"]),
                ("st_xn0", p.counts["st_xn0"]),
                ("st_xn1", p.counts["st_xn1"])]
    for e in ENGINES:
        p.waits(e, barrier2)
    for s in ("ld_wq0", "ld_wq1", "ld3x", "st_h0", "st_h1", "st_h2",
              "ld_h0", "ld_h1", "ld_h2", "ld_h3", "ld_xa", "ld_wd",
              "ld_sw"):
        p.sem(s)

    wgT_v = lambda pc: warena[:, pc * DI:(pc + 1) * DI]
    wuT_v = lambda pc: warena[:, 8 * DI + pc * DI: 8 * DI + (pc + 1) * DI]
    wqf_v = lambda q: xb[:, q * D:(q + 1) * D]
    xnC_v = lambda q: xb[:, q * D:(q + 1) * D]
    hbuf_v = lambda q: xb[:, 2048 + q * 512: 2048 + (q + 1) * 512]
    xq_v = lambda s4: xnbf[:, s4 * D:(s4 + 1) * D]
    sgate_v = lambda q: xn_st[:, q * 512:(q + 1) * 512]
    wint_v = lambda q: xnbf[:, q * D:(q + 1) * D]

    p.wait("tensor", ("gld_w2", p.counts["gld_w2"]))
    p.op("tensor", lambda t: t.matmul(pR0[:, 0:1], ones1[:], clipx_sb[:],
                                      start=True, stop=True))
    tok = p.op("tensor", lambda t: t.matmul(pR0[:, 128:129], ones1[:],
                                            clipd_sb[:], start=True,
                                            stop=True), sig="tg")
    p.wait("scalar", tok)
    p.op("scalar", lambda s: s.activation(sgm_x[:], pR0[:, 0:1], ACT.Sigmoid))
    tok = p.op("scalar", lambda s: s.activation(sgm_d[:], pR0[:, 128:129],
                                                ACT.Sigmoid), sig="sg")
    sgm_tok = tok
    p.wait("gpsimd", x1_cc)
    t2g = p.dma("gpsimd", ams, cc1_out[:], "gld_w2")
    p.wait("tensor", t2g)
    p.wait("tensor", sgm_tok)
    tok = p.op("tensor", lambda t: t.matmul(
        pR0[:, 256:257], ones1[:], ams, start=True, stop=True), sig="tg")
    p.wait("vector", tok)
    x1_tok = p.op("vector", lambda v: v.tensor_copy(gmax_x[:],
                                                    pR0[:, 256:257]),
                  sig="vg")
    p.wait("vector", sgm_tok)
    p.wait("vector", x1_tok)

    def fvv(build):
        tok = p.op("vector", build, sig="fv")
        p.wait("vector", tok)
    fvv(lambda v: v.tensor_tensor(sx_t[:], sgm_x[:], gmax_x[:], ALU.mult))
    fvv(lambda v: v.reciprocal(rs_x[:], sx_t[:]))
    fvv(lambda v: v.tensor_scalar(rs_x[:], rs_x[:], 127.0, None, ALU.mult))

    fr_wq = [None, None]
    wq_ev = None
    for wi, (w_hbm, wT_v, stab) in enumerate(
            ((wgate_hbm, wgT_v, swg_tab), (wup_hbm, wuT_v, swu_tab))):
        wbase = warena[:, wi * 8 * DI:(wi + 1) * 8 * DI]
        for ib in range(NIB):
            q = (wi * NIB + ib) % 2
            tok = p.dma("sync", wqf_v(q), w_hbm[ib * 128:(ib + 1) * 128, :],
                        "ld_wq%d" % q,
                        waits=[t for t in (fr_wq[q],) if t])
            p.wait("vector", tok)
            fvv(lambda v, q=q, stab=stab, ib=ib: v.reduce_max(
                stab[:, ib:ib + 1], wqf_v(q), AX, apply_absolute_value=True))
            fvv(lambda v, stab=stab, ib=ib: v.reciprocal(
                wrw[:], stab[:, ib:ib + 1]))
            fvv(lambda v: v.tensor_scalar(wrw[:], wrw[:], 127.0, None,
                                          ALU.mult))
            fvv(lambda v, q=q: v.tensor_scalar(
                wqf_v(q), wqf_v(q), wrw[:], RND, ALU.mult, ALU.add))
            tok = p.op("vector", lambda v, q=q: v.tensor_scalar(
                wint_v(q), wqf_v(q), RND, None, ALU.subtract), sig="vg")
            fr_wq[q] = tok
            p.wait("tensor", tok)
            if wq_ev:
                p.wait("tensor", wq_ev)
            for c in range(8):
                p.op("tensor", lambda t, c=c, q=q: t.transpose(
                    TRQ[:, c * 128:(c + 1) * 128],
                    wint_v(q)[:, c * 128:(c + 1) * 128], ident[:]),
                    sig="tg" if c == 7 else None)
            ttok = ("tg", p.counts["tg"])
            p.wait("vector", ttok)
            dst = wbase.rearrange(
                "p (pc n) -> p pc n", n=DI)[:, :, ib * 128:(ib + 1) * 128]
            wq_ev = p.op("vector", lambda v, dst=dst: v.tensor_copy(
                dst, TRQ[:].rearrange("p (c n) -> p c n", n=128)), sig="vg")
    p.wait("vector", wq_ev)
    fvv(lambda v: v.tensor_scalar(sg_tab[:], sg_tab[:], sx_t[:],
                                  1.0 / (127.0 * 127.0), ALU.mult, ALU.mult))
    tok = p.op("vector", lambda v: v.tensor_scalar(
        su_tab[:], su_tab[:], sx_t[:], 1.0 / (127.0 * 127.0),
        ALU.mult, ALU.mult), sig="vg")
    sc_tok = tok
    p.wait("vector", sc_tok)
    p.wait("scalar", sc_tok)

    fr_xn = None
    p.waits("sync", [("fv", p.counts["fv"]), ("vg", p.counts["vg"])])
    fr_xq = [None] * 4
    fr_xqfm = None
    fr_h = [None, None, None]
    fr_sg = [None, None, ("vg", p.counts["vg"])]
    hacc_par = None
    fr_xbslot = [None, None]
    for tt in range(NT512):
        t0 = tt * 512
        for s4 in range(2):
            sl = s4 % 2
            p.dma("sync", xb[:, sl * D:(sl + 1) * D],
                  xn_dram[t0 + s4 * 128: t0 + (s4 + 1) * 128, :],
                  "ld3x", waits=[t for t in (fr_xbslot[sl],) if t])
        for s4 in range(4):
            sl = s4 % 2
            p.wait("vector", ("ld3x", p.counts["ld3x"] - (16 if s4 < 2
                                                          else 0)))
            if fr_xq[s4]:
                p.wait("vector", fr_xq[s4])
            src_ap = xb[:, sl * D:(sl + 1) * D]
            fvv(lambda v, src_ap=src_ap: v.tensor_scalar(
                src_ap, src_ap, rs_x[:], QCLIP_HI, ALU.mult, ALU.min))
            fvv(lambda v, src_ap=src_ap: v.tensor_scalar(
                src_ap, src_ap, QCLIP_LO, RND, ALU.max, ALU.add))
            tok = p.op("vector", lambda v, s4=s4, src_ap=src_ap:
                       v.tensor_scalar(xq_v(s4), src_ap, RND, None,
                                       ALU.subtract), sig="vg")
            fr_xbslot[sl] = tok
            if s4 < 2:
                p.dma("sync", xb[:, sl * D:(sl + 1) * D],
                      xn_dram[t0 + (s4 + 2) * 128: t0 + (s4 + 3) * 128, :],
                      "ld3x", waits=[tok])
            p.wait("tensor", tok)
            if fr_xqfm:
                p.wait("tensor", fr_xqfm)
            bank = TRQ if s4 % 2 == 0 else TRK
            for c in range(8):
                p.op("tensor", lambda t, c=c, s4=s4, bank=bank: t.transpose(
                    bank[:, c * 128:(c + 1) * 128],
                    xq_v(s4)[:, c * 128:(c + 1) * 128], ident[:]),
                    sig="tg" if c == 7 else None)
            ttok = ("tg", p.counts["tg"])
            p.wait("vector", ttok)
            dst = xnfm_sb[:, 0:4096].rearrange(
                "p (c s f) -> p c s f", s=4, f=128)[:, :, s4, :]
            tok = p.op("vector", lambda v, dst=dst, bank=bank: v.tensor_copy(
                dst, bank[:].rearrange("p (c f) -> p c f", f=128)), sig="vg")
            fr_xq[s4] = tok
        fr_xn = ("vg", p.counts["vg"])
        xq_done = ("vg", p.counts["vg"])
        p.wait("tensor", xq_done)
        for ib in range(NIB):
            q = ib % 3
            gbank = (pQ0, pQ1, pR0)[q]
            ubank = (pK0, pK1, pR1)[q]
            if fr_sg[q]:
                p.wait("tensor", fr_sg[q])
            for c in range(8):
                lhsg = wgT_v(c)[:, ib * 128:(ib + 1) * 128]
                lhsu = wuT_v(c)[:, ib * 128:(ib + 1) * 128]
                rhs = xnfm_sb[:, c * 512:(c + 1) * 512]
                p.op("tensor", lambda t, lhsg=lhsg, rhs=rhs, gbank=gbank, c=c:
                     t.matmul(gbank[:], lhsg, rhs,
                              start=(c == 0), stop=(c == 7)))
                p.op("tensor", lambda t, lhsu=lhsu, rhs=rhs, ubank=ubank, c=c:
                     t.matmul(ubank[:], lhsu, rhs,
                              start=(c == 0), stop=(c == 7)),
                     sig="tg" if c == 7 else None)
            gu_tok = ("tg", p.counts["tg"])
            p.wait("scalar", gu_tok)
            stok = p.op("scalar", lambda s, q=q, ib=ib, gbank=gbank:
                        s.activation(sgate_v(q), gbank[:], ACT.Silu,
                                     scale=sg_tab[:, ib:ib + 1]), sig="sg")
            p.wait("vector", stok)
            if fr_h[q]:
                p.wait("vector", fr_h[q])
            htok = p.op("vector", lambda v, q=q, ubank=ubank: v.tensor_tensor(
                hbuf_v(q), sgate_v(q), ubank[:], ALU.mult), sig="vg")
            fr_sg[q] = htok
            p.wait("vector", htok)
            fvv(lambda v, q=q: v.reduce_max(wrw[:], hbuf_v(q), AX,
                                            apply_absolute_value=True))
            np_ = (tt % 2)
            if tt == 0:
                tok = p.op("vector", lambda v, ib=ib, np_=np_: v.tensor_copy(
                    hacc[np_][:, ib:ib + 1], wrw[:]), sig="fv")
            else:
                tok = p.op("vector", lambda v, ib=ib, np_=np_:
                           v.scalar_tensor_tensor(
                               hacc[np_][:, ib:ib + 1], wrw[:], 1.0,
                               hacc[1 - np_][:, ib:ib + 1],
                               ALU.mult, ALU.max), sig="fv")
            p.wait("vector", tok)
            p.wait("gpsimd", htok)
            p.dma("gpsimd", h_dram[ib, :, t0:t0 + 512], hbuf_v(q),
                  "st_h%d" % q)
            fr_h[q] = ("st_h%d" % q, p.counts["st_h%d" % q])
        hacc_par = tt % 2
        fr_xqfm = ("tg", p.counts["tg"])

    fvv(lambda v, hp=hacc_par: v.tensor_tensor(hacc[1 - hp][:], hacc[hp][:],
                                               su_tab[:], ALU.mult))
    tok = p.op("vector", lambda v, hp=hacc_par: v.reduce_max(
        am1[:], hacc[1 - hp][:], AX), sig="vg")
    p.wait("gpsimd", tok)
    t1 = p.dma("gpsimd", amx_dram[:], am1[:], "st_o")
    t2 = p.dma("gpsimd", amrow[:],
               amx_dram[:].rearrange("(o p) c -> o (p c)", o=1), "gld_w2",
               waits=[t1])
    p.wait("vector", t2)
    tok = p.op("vector", lambda v: v.reduce_max(ams, amrow, AX),
               sig="vg")
    p.wait("gpsimd", tok)
    t1 = p.dma("gpsimd", cc2_in[:], ams, "st_o")
    p.wait("gpsimd", t1)
    x2_cc = p.op("gpsimd", lambda g: g.collective_compute(
        "AllReduce", ALU.max, replica_groups=[list(range(8))],
        ins=[cc2_in[:]], outs=[cc2_out[:]]), sig="ccs")

    barrier3 = [("vg", p.counts["vg"]), ("sg", p.counts["sg"]),
                ("tg", p.counts["tg"]),
                ("st_h0", p.counts["st_h0"]), ("st_h1", p.counts["st_h1"]),
                ("st_h2", p.counts["st_h2"])]
    for e in ENGINES:
        p.waits(e, barrier3)

    wdT_v = lambda ip: warena[:, ip * D:(ip + 1) * D]
    HQB = 22 * D
    hq_v = lambda ib, par: warena[:, HQB + par * 11264 + ib * 512:
                                  HQB + par * 11264 + (ib + 1) * 512]
    outst_v = lambda q: xb[:, q * D:(q + 1) * D]

    fr_wd = None
    wd_ev = None
    for mb in range(8):
        tok = p.dma("sync", xb[:, 0:DI], wdown_hbm[mb * 128:(mb + 1) * 128, :],
                    "ld_wd", waits=[fr_wd] if fr_wd else [])
        p.wait("vector", tok)
        fvv(lambda v, mb=mb: v.reduce_max(swd_tab[:, mb:mb + 1], xb[:, 0:DI],
                                          AX, apply_absolute_value=True))
        fvv(lambda v, mb=mb: v.reciprocal(wrw[:], swd_tab[:, mb:mb + 1]))
        fvv(lambda v: v.tensor_scalar(wrw[:], wrw[:], 127.0, None, ALU.mult))
        fvv(lambda v, mb=mb: v.tensor_scalar(
            swd_tab[:, mb:mb + 1], swd_tab[:, mb:mb + 1], 1.0 / 127.0, None,
            ALU.mult))
        fvv(lambda v: v.tensor_scalar(xb[:, 0:DI], xb[:, 0:DI], wrw[:], RND,
                                      ALU.mult, ALU.add))
        tok = p.op("vector", lambda v, mb=mb: v.tensor_scalar(
            xnbf[:, 0:DI], xb[:, 0:DI], RND, swd_tab[:, mb:mb + 1],
            ALU.subtract, ALU.mult), sig="vg")
        fr_wd = tok
        p.wait("tensor", tok)
        if wd_ev:
            p.wait("tensor", wd_ev)
        for ip in range(NIB):
            bank = TRQ if ip % 16 < 8 else TRK
            reg = ip % 8
            p.op("tensor", lambda t, ip=ip, bank=bank, reg=reg: t.transpose(
                bank[:, reg * 128:(reg + 1) * 128],
                xnbf[:, ip * 128:(ip + 1) * 128], ident[:]),
                sig="tg" if (ip % 8 == 7 or ip == NIB - 1) else None)
            if ip % 8 == 7 or ip == NIB - 1:
                ttok = ("tg", p.counts["tg"])
                p.wait("vector", ttok)
                lo = (ip // 8) * 8
                n = ip - lo + 1
                dst = warena[:].rearrange(
                    "p (ip n) -> p ip n", n=D)[:, lo:lo + n,
                                               mb * 128:(mb + 1) * 128]
                wd_ev = p.op("vector", lambda v, dst=dst, bank=bank, n=n:
                             v.tensor_copy(dst, bank[:, 0:n * 128].rearrange(
                                 "p (c f) -> p c f", f=128)), sig="vg")
                p.wait("tensor", wd_ev)
    p.wait("gpsimd", x2_cc)
    t2g = p.dma("gpsimd", ams, cc2_out[:], "gld_w2")
    p.wait("tensor", t2g)
    tok = p.op("tensor", lambda t: t.matmul(
        pK1[:, 0:1], ones1[:], ams, start=True, stop=True), sig="tg")
    p.wait("vector", tok)
    x2_tok = p.op("vector", lambda v: v.tensor_copy(gmax_h[:], pK1[:, 0:1]),
                  sig="vg")
    p.wait("vector", x2_tok)
    fvv(lambda v: v.tensor_tensor(sh_t[:], sgm_d[:], gmax_h[:], ALU.mult))
    fvv(lambda v: v.tensor_scalar(sh_t[:], sh_t[:], 1.0 / 127.0, None,
                                  ALU.mult))
    fvv(lambda v: v.reciprocal(rsh_t[:], sh_t[:]))
    tok = p.op("vector", lambda v: v.tensor_scalar(
        rsq_tab[:], rsq_tab[:], rsh_t[:], None, ALU.mult), sig="vg")
    p.wait("vector", tok)
    dscale_tok = tok

    fr_hl = [fr_wd, fr_wd, fr_wd, fr_wd]
    fr_hqmm = [None, None]
    hq_done = [None, None]
    fr_bank = [None, None]
    fr_os = [None, None]
    fr_xares = [None, None]

    hslot = lambda si: (hbuf_v(si) if si < 2
                        else xb[:, 2048 + si * 512:2048 + (si + 1) * 512])

    def emit_quantD(tt):
        par = tt % 2
        t0q = tt * 512
        if fr_hqmm[par]:
            p.wait("vector", fr_hqmm[par])
        for ib0 in range(0, NIB, 2):
            ibs = [ib0] + ([ib0 + 1] if ib0 + 1 < NIB else [])
            toks = []
            for ib in ibs:
                si = ib % 4
                toks.append(p.dma("sync", hslot(si),
                                  h_dram[ib, :, t0q:t0q + 512],
                                  "ld_h%d" % si,
                                  waits=[t for t in (fr_hl[si],) if t]))
            for tk in toks:
                p.wait("vector", tk)
            for ib in ibs:
                si = ib % 4
                tk = p.op("vector", lambda v, si=si, ib=ib: v.tensor_scalar(
                    hslot(si), hslot(si), rsq_tab[:, ib:ib + 1], QCLIP_HI,
                    ALU.mult, ALU.min), sig="fv" if ib == ibs[-1] else None)
            p.wait("vector", tk)
            for ib in ibs:
                si = ib % 4
                tk = p.op("vector", lambda v, si=si: v.tensor_scalar(
                    hslot(si), hslot(si), QCLIP_LO, RND, ALU.max, ALU.add),
                    sig="fv" if ib == ibs[-1] else None)
            p.wait("vector", tk)
            for ib in ibs:
                si = ib % 4
                tk = p.op("vector", lambda v, si=si, ib=ib, par=par:
                          v.tensor_scalar(hq_v(ib, par), hslot(si), RND,
                                          None, ALU.subtract),
                          sig="vg" if ib == ibs[-1] else None)
            fr_hl[ibs[0] % 4] = fr_hl[ibs[-1] % 4] = tk
        hq_done[par] = ("vg", p.counts["vg"])

    emit_quantD(0)
    for tt in range(NT512):
        t0 = tt * 512
        if tt + 1 < NT512:
            emit_quantD(tt + 1)
        par = tt % 2
        p.wait("tensor", hq_done[par])
        for t4 in range(4):
            b2 = t4 % 2
            bA, bB = (pQ0, pQ1) if b2 == 0 else (pK0, pK1)
            if tt == 0 and t4 == 1:
                p.wait("tensor", x2_tok)
            q4l = t4 % 2
            xw = [fr_xares[q4l]] if fr_xares[q4l] else []
            if tt == 0 and t4 == 0:
                xw.append(dscale_tok)
            xatok = p.dma("scalar", xn_st[:, q4l * D:(q4l + 1) * D],
                          xa_dram[t0 + t4 * 128:t0 + (t4 + 1) * 128, :],
                          "ld_xa", waits=xw)
            if fr_bank[b2]:
                p.wait("tensor", fr_bank[b2])
            for ib in range(NIB):
                lhs = hq_v(ib, par)[:, t4 * 128:(t4 + 1) * 128]
                p.op("tensor", lambda t, lhs=lhs: t.ldweights(lhs))
                p.op("tensor", lambda t, ib=ib, lhs=lhs, bA=bA: t.matmul(
                    bA[:], lhs, wdT_v(ib)[:, 0:512],
                    start=(ib == 0), stop=(ib == NIB - 1)))
                p.op("tensor", lambda t, ib=ib, lhs=lhs, bB=bB: t.matmul(
                    bB[:], lhs, wdT_v(ib)[:, 512:1024],
                    start=(ib == 0), stop=(ib == NIB - 1)),
                    sig="tg" if ib == NIB - 1 else None)
            dtok = ("tg", p.counts["tg"])
            p.wait("vector", dtok)
            p.wait("vector", xatok)
            q4 = t4 % 2
            if fr_os[q4]:
                p.wait("vector", fr_os[q4])
            xa_ap = xn_st[:, q4 * D:(q4 + 1) * D]
            oap = outst_v(q4)
            fvv(lambda v, oap=oap, bA=bA: v.tensor_scalar(
                oap[:, 0:512], bA[:], sh_t[:], None, ALU.mult))
            fvv(lambda v, oap=oap, bB=bB: v.tensor_scalar(
                oap[:, 512:1024], bB[:], sh_t[:], None, ALU.mult))
            tok = p.op("vector", lambda v, xa_ap=xa_ap, oap=oap:
                       v.tensor_tensor(oap, oap, xa_ap, ALU.add), sig="vg")
            fr_bank[b2] = tok
            fr_xares[q4] = tok
            p.wait("gpsimd", tok)
            p.dma("gpsimd", out_hbm[t0 + t4 * 128:t0 + (t4 + 1) * 128, :],
                  oap, "st_o")
            fr_os[q4] = ("st_o", p.counts["st_o"])
        fr_hqmm[par] = ("tg", p.counts["tg"])

    for s in ("st_o", "st_xn0", "st_xn1", "st_qk", "st_fm", "st_xa0",
              "st_xa1", "st_h0", "st_h1", "st_h2"):
        p.wait("gpsimd", (s, p.counts[s]))

    def finalize():
        with nc.Block() as block:
            p.finalize(block)
    return nc, finalize, dict(nb=nb, nch=nch)


def kernel(**inputs):
    NB_TOT, T = 32, 2048
    nb, nch = 4, 16
    x = np.ascontiguousarray(np.asarray(inputs["L0_attn_x"], np.float32))
    x = x.reshape(NB_TOT * T, D)
    nc, finalize, _ = build_program(nb, nch)
    finalize()
    base = {
        "wq": np.ascontiguousarray(inputs["L0_attn_Wq"], np.float32),
        "wk": np.ascontiguousarray(inputs["L0_attn_Wk"], np.float32),
        "wv": np.ascontiguousarray(inputs["L0_attn_Wv"], np.float32),
        "wg": np.ascontiguousarray(inputs["L0_attn_Wg"], np.float32),
        "wo": np.ascontiguousarray(inputs["L0_attn_Wo"], np.float32),
        "gnw": np.ascontiguousarray(inputs["L0_attn_g_norm_w"],
                                    np.float32).reshape(1, DV),
        "clipx": np.ascontiguousarray(inputs["L0_ffn_clip_x"],
                                      np.float32).reshape(1, 1),
        "clipd": np.ascontiguousarray(inputs["L0_ffn_clip_down"],
                                      np.float32).reshape(1, 1),
        "wgate": np.ascontiguousarray(inputs["L0_ffn_w_gate"], np.float32),
        "wup": np.ascontiguousarray(inputs["L0_ffn_w_up"], np.float32),
        "wdown": np.ascontiguousarray(inputs["L0_ffn_w_down"], np.float32),
    }
    ntok = nb * T
    in_maps = [dict(base, x=x[c * ntok:(c + 1) * ntok]) for c in range(8)]
    res = run_bass_kernel_spmd(nc, in_maps, core_ids=list(range(8)))
    out = np.concatenate([r["out"] for r in res.results], 0)
    return out.reshape(NB_TOT, T, D)
